# version 11
# baseline (speedup 1.0000x reference)
import os
import numpy as np
from contextlib import ExitStack

import concourse.bass as bass
import concourse.mybir as mybir
from concourse.bass_utils import run_bass_kernel_spmd
from concourse.alu_op_type import AluOpType as OP

F32 = mybir.dt.float32
BF16 = mybir.dt.bfloat16
I32 = mybir.dt.int32
AF = mybir.ActivationFunctionType
AX = mybir.AxisListType.X

B, T, DM = 2, 4096, 2048
H, KVH, HD = 16, 4, 128
NREP = H // KVH
QB = 512
CH = 512
ASTEPS, BSTEPS = 4, 8
STEPS = ASTEPS + BSTEPS
KCH = DM // 128
SCALE = float(1.0 / np.sqrt(np.float32(HD)))
NEG = -1.0e30
EPS = 1e-5
FI = 8192
NTOK = 1024


class S:
    def __init__(self, nc, name):
        self.h = nc._ctx.enter_context(nc.semaphore(name))
        self.n = 0

    def inc(self, instr, k=1):
        instr.then_inc(self.h, k)
        self.n += k
        return instr

    def wait(self, eng, n=None):
        eng.wait_ge(self.h, self.n if n is None else n)


def build_nc():
    nc = bass.Bass("TRN2", target_bir_lowering=False, debug=False)
    ctx = ExitStack()
    nc._ctx = ctx

    d_xq = nc.dram_tensor("xq_rows", [NTOK, DM], F32, kind="ExternalInput").ap()
    d_xkv = nc.dram_tensor("x_kv", [STEPS * CH, DM], F32, kind="ExternalInput").ap()
    d_wq = nc.dram_tensor("wq", [DM, DM], F32, kind="ExternalInput").ap()
    d_wk = nc.dram_tensor("wk", [DM, KVH * HD], F32, kind="ExternalInput").ap()
    d_wv = nc.dram_tensor("wv", [DM, KVH * HD], F32, kind="ExternalInput").ap()
    d_wo = nc.dram_tensor("wo", [DM, DM], F32, kind="ExternalInput").ap()
    d_wgT = nc.dram_tensor("wgT", [DM, FI], F32, kind="ExternalInput").ap()
    d_wuT = nc.dram_tensor("wuT", [DM, FI], F32, kind="ExternalInput").ap()
    d_wdT = nc.dram_tensor("wdT", [FI, DM], F32, kind="ExternalInput").ap()
    d_masks = nc.dram_tensor("masks", [4 * 128, CH], BF16, kind="ExternalInput").ap()
    d_gates = nc.dram_tensor("gates", [128, STEPS], F32, kind="ExternalInput").ap()
    d_ident = nc.dram_tensor("ident", [128, 128], BF16, kind="ExternalInput").ap()
    d_out = nc.dram_tensor("out", [NTOK, DM], F32, kind="ExternalOutput").ap()
    d_hsp = nc.dram_tensor("h_spill", [NTOK, FI], F32, kind="Internal").ap()
    d_rsp = nc.dram_tensor("res_spill", [NTOK, DM], F32, kind="Internal").ap()

    TE, VE, SE, SY, GP = nc.tensor, nc.vector, nc.scalar, nc.sync, nc.gpsimd

    s_cst = S(nc, "s_cst")
    s_x = [S(nc, "s_x0"), S(nc, "s_x1")]
    s_wab = [S(nc, "s_wab0"), S(nc, "s_wab1")]
    s_wq = [S(nc, "s_wq0"), S(nc, "s_wq1")]
    s_fa = [S(nc, "s_fa0"), S(nc, "s_fa1")]
    st_h = [S(nc, "st_h0"), S(nc, "st_h1")]
    st_x = [S(nc, "st_x0"), S(nc, "st_x1")]
    s_xf = [S(nc, "s_xf0"), S(nc, "s_xf1")]
    t2v = S(nc, "t2v")
    v2t = S(nc, "v2t")
    sc2 = S(nc, "sc2")
    vc2 = S(nc, "vc2")
    gp_done = S(nc, "gp_done")
    a_sc = [S(nc, "a_sc0"), S(nc, "a_sc1"), S(nc, "a_sc2")]
    a_bf = [S(nc, "a_bf0"), S(nc, "a_bf1"), S(nc, "a_bf2")]
    a_vb = S(nc, "a_vb")
    a_sp = S(nc, "a_sp")
    a_ph = S(nc, "a_ph")
    a_pl = S(nc, "a_pl")
    a_pt = [S(nc, "a_pt0"), S(nc, "a_pt1"), S(nc, "a_pt2")]
    a_pe = [S(nc, "a_pe0"), S(nc, "a_pe1"), S(nc, "a_pe2")]
    spl_v = S(nc, "spl_v")
    spl_s = S(nc, "spl_s")
    tr_done = [S(nc, "trd0"), S(nc, "trd1")]
    tr_free = [S(nc, "trf0"), S(nc, "trf1")]
    pa_done = [S(nc, "pad0"), S(nc, "pad1")]
    pa_free = [S(nc, "paf0"), S(nc, "paf1")]

    x_sb = ctx.enter_context(nc.sbuf_tensor("x_sb", [128, 2 * DM], F32))
    ident_s = ctx.enter_context(nc.sbuf_tensor("ident_s", [128, 128], BF16))
    gates_s = ctx.enter_context(nc.sbuf_tensor("gates_s", [128, STEPS], F32))
    ones_b = ctx.enter_context(nc.sbuf_tensor("ones_b", [128, 128], BF16))
    bn6 = ctx.enter_context(nc.sbuf_tensor("bn6", [128, 24], F32))
    bn2 = ctx.enter_context(nc.sbuf_tensor("bn2", [128, 2], F32))
    lnsc = ctx.enter_context(nc.sbuf_tensor("lnsc", [128, 8], F32))
    wabs = ctx.enter_context(nc.sbuf_tensor("wabs", [128, 2 * 1024], F32))
    asum = ctx.enter_context(nc.sbuf_tensor("asum", [128, 8], F32))
    wsml = ctx.enter_context(nc.sbuf_tensor("wsml", [128, 16], F32))
    wabs_t = ctx.enter_context(nc.sbuf_tensor("wabs_t", [128, 8], BF16))
    wacc = ctx.enter_context(nc.sbuf_tensor("wacc", [128, 384], F32))

    tile_ctr = [0]
    unit_ctr = [0]
    pbar = S(nc, "pbar")

    def bar():
        for eng in (TE, VE, SE, GP, SY):
            pbar.inc(eng.drain())
        for eng in (TE, VE, SE, GP, SY):
            pbar.wait(eng)

    STAGE_LIMIT = int(os.environ.get("KB_STAGE", "99"))
    stage_ctr = [0]

    def stage_done():
        stage_ctr[0] += 1
        if stage_ctr[0] < STAGE_LIMIT:
            return False
        bar()
        fin = S(nc, "fin")
        fin.inc(SY.dma_start(out=d_out[0:128, :], in_=x_sb[:, 0:DM]), 16)
        fin.wait(SY)
        return True


    def ln1_subtile(src_rows, banks, done_s, free_s, ln_h, ln_l, emit_dst):
        slot = tile_ctr[0] % 2
        tile_ctr[0] += 1
        sl = x_sb[:, slot * DM:(slot + 1) * DM]
        xf = s_xf[slot]
        xf.wait(SY)
        s_x[slot].inc(SY.dma_start(out=sl, in_=src_rows), 16)
        s_x[slot].wait(VE)
        for c in range(4):
            VE.bn_stats(bn6[:, 6 * c:6 * (c + 1)], sl[:, 512 * c:512 * (c + 1)])
        VE.drain()
        VE.bn_aggr(bn2[:, :], bn6[:, :])
        VE.drain()
        vc2.inc(VE.tensor_scalar(lnsc[:, 0:1], bn2[:, 0:1], -1.0, None, op0=OP.mult))
        vc2.wait(SE)
        sc2.inc(SE.activation(lnsc[:, 1:2], bn2[:, 1:2], AF.Sqrt, bias=lnsc[:, 7:8], scale=1.0))
        sc2.wait(VE)
        VE.reciprocal(lnsc[:, 2:3], lnsc[:, 1:2])
        VE.drain()
        spl_v.inc(VE.tensor_scalar(sl, sl, lnsc[:, 0:1], lnsc[:, 2:3], op0=OP.add, op1=OP.mult))
        released = False
        nb = len(banks)
        for half in range(2):
            hsl = sl[:, half * 1024:(half + 1) * 1024]
            spl_v.wait(SE)
            spl_s.inc(SE.copy(ln_h[:, :], hsl))
            spl_s.wait(VE)
            spl_v.inc(VE.tensor_tensor(ln_l[:, :], hsl, ln_h[:, :], op=OP.subtract))
            if half == 1:
                xf.inc(VE.engine_nop(), 1)
                released = True
            for is_lo, srcb in ((0, ln_h), (1, ln_l)):
                bi = (2 * half + is_lo) % nb
                free_s[bi].wait(TE)
                if is_lo == 0:
                    spl_s.wait(TE)
                else:
                    spl_v.wait(TE)
                last = None
                for i in range(8):
                    last = TE.transpose(banks[bi][:, i * 128:(i + 1) * 128],
                                        srcb[:, i * 128:(i + 1) * 128], ident_s[:, :])
                done_s[bi].inc(last)
                eng = VE if (2 * half + is_lo) % 2 == 0 else SE
                done_s[bi].wait(eng)
                emit_dst(eng, banks[bi], half * 8, 8, is_lo)
                if eng is VE:
                    spl_v.inc(VE.engine_nop())
                    free_s[bi].inc(VE.engine_nop())
                else:
                    spl_s.inc(SE.drain())
                    free_s[bi].inc(SE.drain())
        assert released

    attn_ctx = ExitStack()
    wk_h = attn_ctx.enter_context(nc.sbuf_tensor("wk_h", [128, KCH * 512], BF16))
    wk_l = attn_ctx.enter_context(nc.sbuf_tensor("wk_l", [128, KCH * 512], BF16))
    wv_h = attn_ctx.enter_context(nc.sbuf_tensor("wv_h", [128, KCH * 512], BF16))
    wv_l = attn_ctx.enter_context(nc.sbuf_tensor("wv_l", [128, KCH * 512], BF16))
    q_h = attn_ctx.enter_context(nc.sbuf_tensor("q_h", [128, H * 512], BF16))
    q_l = attn_ctx.enter_context(nc.sbuf_tensor("q_l", [128, H * 512], BF16))
    ln_h = attn_ctx.enter_context(nc.sbuf_tensor("ln_h", [128, 1024], BF16))
    ln_l = attn_ctx.enter_context(nc.sbuf_tensor("ln_l", [128, 1024], BF16))
    lnf_h = attn_ctx.enter_context(nc.sbuf_tensor("lnf_h", [128, DM], BF16))
    lnf_l = attn_ctx.enter_context(nc.sbuf_tensor("lnf_l", [128, DM], BF16))
    masks_s = attn_ctx.enter_context(nc.sbuf_tensor("masks_s", [128, 4 * CH], BF16))
    p_f = attn_ctx.enter_context(nc.sbuf_tensor("p_f", [128, 3 * CH], F32))
    p_h = attn_ctx.enter_context(nc.sbuf_tensor("p_h", [128, 3 * CH], BF16))
    p_l = attn_ctx.enter_context(nc.sbuf_tensor("p_l", [128, 3 * CH], BF16))
    pt_hl = attn_ctx.enter_context(nc.sbuf_tensor("pt_hl", [128, 6 * CH], BF16))
    m_run = attn_ctx.enter_context(nc.sbuf_tensor("m_run", [128, 128], F32))
    sden = attn_ctx.enter_context(nc.sbuf_tensor("sden", [128, 128], F32))
    cmx = attn_ctx.enter_context(nc.sbuf_tensor("cmx", [128, 64], F32))
    alph = attn_ctx.enter_context(nc.sbuf_tensor("alph", [128, 64], F32))
    csum = attn_ctx.enter_context(nc.sbuf_tensor("csum", [128, 64], F32))
    bi_p = attn_ctx.enter_context(nc.sbuf_tensor("bi_p", [128, 64], F32))
    bi_a = attn_ctx.enter_context(nc.sbuf_tensor("bi_a", [128, 64], F32))
    rden = attn_ctx.enter_context(nc.sbuf_tensor("rden", [128, 64], F32))

    s_cst.inc(SY.dma_start(out=ident_s[:, :], in_=d_ident), 16)
    s_cst.inc(SY.dma_start(out=gates_s[:, :], in_=d_gates), 16)
    for t in range(4):
        s_cst.inc(SY.dma_start(out=masks_s[:, t * CH:(t + 1) * CH],
                               in_=d_masks[t * 128:(t + 1) * 128, :]), 16)
    s_cst.wait(TE)
    s_cst.wait(VE)
    s_cst.wait(SE)
    GP.memset(ones_b[:, :], 1.0)
    gp_done.inc(GP.engine_nop())
    VE.memset(asum[:, :], 0.0)
    VE.memset(lnsc[:, 7:8], EPS)
    vc2.inc(VE.drain())

    wkv_marks = []
    idx = 0
    for dsrc, dsth, dstl in ((d_wk, wk_h, wk_l), (d_wv, wv_h, wv_l)):
        for k in range(KCH):
            slot = idx % 2
            sl = wabs[:, slot * 1024:slot * 1024 + 512]
            if idx >= 2:
                SY.wait_ge(vc2.h, wkv_marks[idx - 2])
            s_wab[slot].inc(SY.dma_start(out=sl, in_=dsrc[k * 128:(k + 1) * 128, :]), 16)
            s_wab[slot].wait(SE)
            cols = slice(k * 512, (k + 1) * 512)
            sc2.inc(SE.copy(dsth[:, cols], sl))
            sc2.wait(VE)
            vc2.inc(VE.tensor_tensor(dstl[:, cols], sl, dsth[:, cols], op=OP.subtract))
            wkv_marks.append(vc2.n)
            idx += 1
    post_wkv_v = vc2.n
    if stage_done():
        return nc

    scale_tiles = []
    for wi, d in enumerate((d_wgT, d_wuT, d_wdT)):
        rows, cols = (DM, FI) if wi < 2 else (FI, DM)
        for r in range(rows // 128):
            for c in range(cols // 1024):
                scale_tiles.append((wi, d[r * 128:(r + 1) * 128,
                                          c * 1024:(c + 1) * 1024]))
    assert len(scale_tiles) == 384
    sp_state = {"i": 0, "marks": [], "vmarks": []}

    def emit_scale_tiles(n):
        for _ in range(n):
            i = sp_state["i"]
            if i >= len(scale_tiles):
                return
            wi, src = scale_tiles[i]
            slot = i % 2
            sl = wabs[:, slot * 1024:(slot + 1) * 1024]
            if i >= 2:
                SE.wait_ge(sc2.h, sp_state["marks"][i - 2])
            else:
                SE.wait_ge(vc2.h, post_wkv_v)
            s_wab[slot].inc(SE.dma_start(out=sl, in_=src), 16)
            s_wab[slot].wait(SE)
            sc2.inc(SE.activation(sl, sl, AF.Abs, accum_out=wacc[:, i:i + 1]))
            sp_state["marks"].append(sc2.n)
            sp_state["i"] += 1

    def q_build2(phase):
        qctx = ExitStack()
        lnq_h = qctx.enter_context(nc.sbuf_tensor(f"lnq_h{phase}", [128, KCH * 512], BF16))
        lnq_l = qctx.enter_context(nc.sbuf_tensor(f"lnq_l{phase}", [128, KCH * 512], BF16))
        qps_ctx = ExitStack()
        qtr = [qps_ctx.enter_context(
            nc.psum_tensor(f"qtr{phase}_{i}", [128, 1024], BF16)) for i in range(2)]
        for sub in range(4):
            r0 = phase * QB + sub * 128

            def emitq(eng, bank, kbase, take, is_lo, sub=sub):
                dstb = lnq_l if is_lo else lnq_h
                for i in range(take):
                    col = (kbase + i) * 512 + sub * 128
                    if eng is VE:
                        VE.tensor_copy(dstb[:, col:col + 128],
                                       bank[:, i * 128:(i + 1) * 128])
                    else:
                        SE.copy(dstb[:, col:col + 128],
                                bank[:, i * 128:(i + 1) * 128])
            ln1_subtile(d_xq[r0:r0 + 128, :], qtr, [tr_done[0], tr_done[1]],
                        [tr_free[0], tr_free[1]], ln_h, ln_l, emitq)
        bar()
        qps_ctx.close()
        wctx = ExitStack()
        wqf = wctx.enter_context(nc.sbuf_tensor(f"wqf{phase}", [128, 2 * 768], F32))
        wqh = wctx.enter_context(nc.sbuf_tensor(f"wqh{phase}", [128, 2 * 768], BF16))
        wql = wctx.enter_context(nc.sbuf_tensor(f"wql{phase}", [128, 2 * 768], BF16))
        qps_ctx2 = ExitStack()
        qps = [qps_ctx2.enter_context(
            nc.psum_tensor(f"qps{phase}_{i}", [128, 512], F32)) for i in range(6)]
        spl_v.wait(TE)
        spl_s.wait(TE)
        li = 0
        wq_marks = []
        heads_done = 0
        while heads_done < H:
            nh = min(6, H - heads_done)
            for k in range(KCH):
                slot = li % 2
                wsl = wqf[:, slot * 768:slot * 768 + nh * 128]
                if li >= 2:
                    SY.wait_ge(t2v.h, wq_marks[li - 2])
                s_wq[slot].inc(SY.dma_start(
                    out=wsl,
                    in_=d_wq[k * 128:(k + 1) * 128,
                             heads_done * 128:(heads_done + nh) * 128]), 16)
                s_wq[slot].wait(SE)
                hcols = slice(slot * 768, slot * 768 + nh * 128)
                spl_s.inc(SE.copy(wqh[:, hcols], wsl))
                spl_s.wait(VE)
                spl_v.inc(VE.tensor_tensor(wql[:, hcols], wsl, wqh[:, hcols], op=OP.subtract))
                spl_v.wait(TE)
                spl_s.wait(TE)
                for hh in range(nh):
                    hc = slot * 768 + hh * 128
                    wh = wqh[:, hc:hc + 128]
                    wl = wql[:, hc:hc + 128]
                    TE.matmul(qps[hh][:, :], wh, lnq_h[:, k * 512:(k + 1) * 512],
                              start=(k == 0), stop=False)
                    TE.matmul(qps[hh][:, :], wh, lnq_l[:, k * 512:(k + 1) * 512],
                              start=False, stop=False)
                    last = TE.matmul(qps[hh][:, :], wl, lnq_h[:, k * 512:(k + 1) * 512],
                                     start=False, stop=(k == KCH - 1))
                t2v.inc(last)
                wq_marks.append(t2v.n)
                li += 1
            t2v.wait(SE)
            last = None
            for hh in range(nh):
                cols = slice((heads_done + hh) * 512, (heads_done + hh + 1) * 512)
                last = SE.copy(q_h[:, cols], qps[hh][:, :])
            spl_s.inc(last)
            spl_s.wait(VE)
            t2v.wait(VE)
            last = None
            for hh in range(nh):
                cols = slice((heads_done + hh) * 512, (heads_done + hh + 1) * 512)
                last = VE.tensor_tensor(q_l[:, cols], qps[hh][:, :], q_h[:, cols],
                                        op=OP.subtract)
            spl_v.inc(last)
            v2t.inc(VE.engine_nop())
            v2t.wait(TE)
            heads_done += nh
        bar()
        qps_ctx2.close()
        wctx.close()
        qctx.close()
        return spl_v.n, spl_s.n

    def kv_chunk(phase, step, kv_bufs, aps):
        (k_h, k_l, v_h, v_l) = kv_bufs
        (sc_ps, pt_ps, pa_ps) = aps
        base = (0 if phase == 0 else ASTEPS) + step
        TE.wait_ge(vc2.h, vc2.n)
        for tt in range(4):
            r0 = base * CH + tt * 128

            def emitk(eng, bank, kbase, take, is_lo):
                dstb = lnf_l if is_lo else lnf_h
                if eng is VE:
                    VE.tensor_copy(dstb[:, kbase * 128:(kbase + take) * 128],
                                   bank[:, :take * 128])
                else:
                    SE.copy(dstb[:, kbase * 128:(kbase + take) * 128],
                            bank[:, :take * 128])
            ln1_subtile(d_xkv[r0:r0 + 128, :], pt_ps, [a_pt[0], a_pt[0]],
                        [a_pe[0], a_pe[0]], ln_h, ln_l, emitk)
            spl_v.wait(TE)
            spl_s.wait(TE)
            for g in range(KVH):
                bi = g % 2
                pa_free[bi].wait(TE)
                for k in range(KCH):
                    wcol = slice(k * 512 + g * 128, k * 512 + (g + 1) * 128)
                    lncol = slice(k * 128, (k + 1) * 128)
                    TE.matmul(pa_ps[bi][:, :128], wk_h[:, wcol], lnf_h[:, lncol],
                              start=(k == 0), stop=False)
                    TE.matmul(pa_ps[bi][:, :128], wk_h[:, wcol], lnf_l[:, lncol],
                              start=False, stop=False)
                    last = TE.matmul(pa_ps[bi][:, :128], wk_l[:, wcol], lnf_h[:, lncol],
                                     start=False, stop=(k == KCH - 1))
                pa_done[bi].inc(last)
                kcols = slice(g * 512 + tt * 128, g * 512 + (tt + 1) * 128)
                pa_done[bi].wait(SE)
                spl_s.inc(SE.copy(k_h[:, kcols], pa_ps[bi][:, :128]))
                spl_s.wait(VE)
                spl_v.inc(VE.tensor_tensor(k_l[:, kcols], pa_ps[bi][:, :128],
                                           k_h[:, kcols], op=OP.subtract))
                pa_free[bi].inc(VE.engine_nop())
            bi = tt % 2
            pa_free[bi].wait(TE)
            for k in range(KCH):
                lncol = slice(k * 128, (k + 1) * 128)
                wcols = slice(k * 512, (k + 1) * 512)
                TE.matmul(pa_ps[bi][:, :], lnf_h[:, lncol], wv_h[:, wcols],
                          start=(k == 0), stop=False)
                TE.matmul(pa_ps[bi][:, :], lnf_h[:, lncol], wv_l[:, wcols],
                          start=False, stop=False)
                last = TE.matmul(pa_ps[bi][:, :], lnf_l[:, lncol], wv_h[:, wcols],
                                 start=False, stop=(k == KCH - 1))
            pa_done[bi].inc(last)
            vcols = slice(tt * 512, (tt + 1) * 512)
            pa_done[bi].wait(SE)
            spl_s.inc(SE.copy(v_h[:, vcols], pa_ps[bi][:, :]))
            spl_s.wait(VE)
            spl_v.inc(VE.tensor_tensor(v_l[:, vcols], pa_ps[bi][:, :], v_h[:, vcols],
                                       op=OP.subtract))
            pa_free[bi].inc(VE.engine_nop())
        return spl_v.n, spl_s.n

    ALVL = int(os.environ.get("KB_ALVL", "9"))

    AMARKS = {"sc": {}, "vb": {}, "sp": {}, "ph": {}, "pl": {},
              "tr": {}, "ev": {}, "av": {}, "o": {}}

    def attend_step(phase, step, kv_bufs, aps, o_acc, kv_v, kv_s, q_v, q_s):
        (k_h, k_l, v_h, v_l) = kv_bufs
        (sc_ps, pt_ps, pa_ps) = aps
        NB = len(sc_ps)
        NC = len(pt_ps)
        par_in = step % 2
        par_out = (step + 1) % 2
        gcol = (0 if phase == 0 else ASTEPS) + step
        gate_col = gates_s[:, gcol:gcol + 1]
        n = 64
        u0 = unit_ctr[0]
        unit_ctr[0] += n
        M = AMARKS
        ln_ev_mark = (a_pe[0], a_pe[0].n)
        first = [True]

        def wai(eng, mark):
            if mark is not None:
                eng.wait_ge(mark[0].h, mark[1])

        def st_scores(i):
            u = u0 + i
            b = u % NB
            q, h = divmod(i, H)
            g = h // NREP
            wai(TE, M["o"].get(u - NB))
            if first[0]:
                TE.wait_ge(spl_v.h, max(kv_v, q_v))
                TE.wait_ge(spl_s.h, max(kv_s, q_s))
                first[0] = False
            qh = q_h[:, h * 512 + q * 128: h * 512 + (q + 1) * 128]
            ql = q_l[:, h * 512 + q * 128: h * 512 + (q + 1) * 128]
            TE.matmul(sc_ps[b][:, :], qh, k_h[:, g * 512:(g + 1) * 512],
                      start=True, stop=False)
            TE.matmul(sc_ps[b][:, :], qh, k_l[:, g * 512:(g + 1) * 512],
                      start=False, stop=False)
            last = TE.matmul(sc_ps[b][:, :], ql, k_h[:, g * 512:(g + 1) * 512],
                             start=False, stop=True)
            a_sc[0].inc(last)
            M["sc"][u] = (a_sc[0], a_sc[0].n)

        def st_stats(i):
            u = u0 + i
            b = u % NB
            q, h = divmod(i, H)
            uc = i
            ucs = slice(uc, uc + 1)
            mi = m_run[:, par_in * 64 + uc: par_in * 64 + uc + 1]
            mo = m_run[:, par_out * 64 + uc: par_out * 64 + uc + 1]
            wai(VE, M["sc"][u])
            if step == 0:
                VE.tensor_tensor(sc_ps[b][:, :], sc_ps[b][:, :],
                                 masks_s[:, q * CH:(q + 1) * CH], op=OP.add)
                VE.drain()
            VE.reduce_max(cmx[:, ucs], sc_ps[b][:, :], axis=AX)
            VE.drain()
            VE.scalar_tensor_tensor(mo, cmx[:, ucs], gate_col, mi,
                                    op0=OP.add, op1=OP.max)
            VE.drain()
            VE.scalar_tensor_tensor(bi_p[:, ucs], mo, -SCALE, gate_col,
                                    op0=OP.mult, op1=OP.add)
            a_vb.inc(VE.tensor_scalar(bi_a[:, ucs], mo, -SCALE, None, op0=OP.mult))
            M["vb"][u] = (a_vb, a_vb.n)

        def st_exp(i):
            u = u0 + i
            b = u % NB
            ps = u % 3
            uc = i
            ucs = slice(uc, uc + 1)
            mi = m_run[:, par_in * 64 + uc: par_in * 64 + uc + 1]
            wai(SE, M["vb"][u])
            wai(SE, M["tr"].get(u - 3))
            SE.activation(alph[:, ucs], mi, AF.Exp, bias=bi_a[:, ucs], scale=SCALE)
            a_ph.inc(SE.activation(p_h[:, ps * CH:(ps + 1) * CH], sc_ps[b][:, :],
                                   AF.Exp, bias=bi_p[:, ucs], scale=SCALE))
            M["ph"][u] = (a_ph, a_ph.n)
            a_sp.inc(SE.activation(p_f[:, ps * CH:(ps + 1) * CH], sc_ps[b][:, :],
                                   AF.Exp, bias=bi_p[:, ucs], scale=SCALE,
                                   accum_out=csum[:, ucs]))
            M["sp"][u] = (a_sp, a_sp.n)

        def st_plo(i):
            u = u0 + i
            ps = u % 3
            uc = i
            ucs = slice(uc, uc + 1)
            wai(GP, M["ph"][u])
            wai(GP, M["sp"][u])
            a_pl.inc(GP.tensor_tensor(p_l[:, ps * CH:(ps + 1) * CH],
                                      p_f[:, ps * CH:(ps + 1) * CH],
                                      p_h[:, ps * CH:(ps + 1) * CH],
                                      op=OP.subtract))
            M["pl"][u] = (a_pl, a_pl.n)
            wai(VE, M["sp"][u])
            vc2.inc(VE.scalar_tensor_tensor(
                sden[:, par_out * 64 + uc: par_out * 64 + uc + 1],
                sden[:, par_in * 64 + uc: par_in * 64 + uc + 1],
                alph[:, ucs], csum[:, ucs], op0=OP.mult, op1=OP.add))

        def st_trans(i):
            u = u0 + i
            c = u % NC
            ps = u % 3
            wai(TE, M["ev"].get(u - NC))
            wai(TE, ln_ev_mark)
            wai(TE, M["ph"][u])
            wai(TE, M["pl"][u])
            for k in range(4):
                TE.transpose(pt_ps[c][:, k * 128:(k + 1) * 128],
                             p_h[:, ps * CH + k * 128: ps * CH + (k + 1) * 128],
                             ident_s[:, :])
            last = None
            for k in range(4):
                last = TE.transpose(pt_ps[c][:, 512 + k * 128: 512 + (k + 1) * 128],
                                    p_l[:, ps * CH + k * 128: ps * CH + (k + 1) * 128],
                                    ident_s[:, :])
            a_pt[0].inc(last)
            M["tr"][u] = (a_pt[0], a_pt[0].n)

        def st_evac(i):
            u = u0 + i
            c = u % NC
            ps = u % 3
            wai(SE, M["tr"][u])
            wai(SE, M["av"].get(u - 3))
            a_pe[0].inc(SE.copy(pt_hl[:, ps * 1024:(ps + 1) * 1024],
                                pt_ps[c][:, :]))
            M["ev"][u] = (a_pe[0], a_pe[0].n)

        def st_av(i):
            u = u0 + i
            b = u % NB
            ps = u % 3
            q, h = divmod(i, H)
            g = h // NREP
            wai(TE, M["ev"][u])
            wai(TE, M["sp"][u])
            last = None
            for k in range(4):
                pth = pt_hl[:, ps * 1024 + k * 128: ps * 1024 + (k + 1) * 128]
                ptl = pt_hl[:, ps * 1024 + 512 + k * 128: ps * 1024 + 512 + (k + 1) * 128]
                vh = v_h[:, k * 512 + g * 128: k * 512 + (g + 1) * 128]
                vl = v_l[:, k * 512 + g * 128: k * 512 + (g + 1) * 128]
                TE.matmul(sc_ps[b][:, :128], pth, vh, start=(k == 0), stop=False)
                TE.matmul(sc_ps[b][:, :128], pth, vl, start=False, stop=False)
                last = TE.matmul(sc_ps[b][:, :128], ptl, vh, start=False,
                                 stop=(k == 3))
            t2v.inc(last)
            M["av"][u] = (t2v, t2v.n)

        def st_oupd(i):
            u = u0 + i
            b = u % NB
            q, h = divmod(i, H)
            uc = i
            ucs = slice(uc, uc + 1)
            wai(VE, M["av"][u])
            osl = o_acc[:, q * DM + h * 128: q * DM + (h + 1) * 128]
            vc2.inc(VE.scalar_tensor_tensor(osl, osl, alph[:, ucs],
                                            sc_ps[b][:, :128],
                                            op0=OP.mult, op1=OP.add))
            M["o"][u] = (vc2, vc2.n)

        for cyc in range(n + 7):
            for off, fn in ((6, st_oupd), (5, st_evac), (5, st_av), (4, st_plo),
                            (4, st_trans), (2, st_exp), (1, st_stats), (0, st_scores)):
                j = cyc - off
                if 0 <= j < n:
                    fn(j)

    def wo_phase(phase, nsteps, o_acc):
        par_fin = nsteps % 2
        vc2.wait(VE)
        VE.reciprocal(rden[:, :64], sden[:, par_fin * 64: par_fin * 64 + 64])
        vc2.inc(VE.drain())
        otr_ctx = ExitStack()
        otr = [otr_ctx.enter_context(
            nc.psum_tensor(f"otr{phase}_{i}", [128, 1024], BF16)) for i in range(2)]
        tr_marks = []
        for q in range(4):
            for h in range(H):
                i = q * H + h
                uc = i
                osl = o_acc[:, q * DM + h * 128: q * DM + (h + 1) * 128]
                pslot = i % 4
                pcol = slice(pslot * 128, pslot * 128 + 128)
                if i >= 4:
                    VE.wait_ge(t2v.h, tr_marks[i - 4])
                VE.tensor_scalar(p_h[:, pcol], osl, rden[:, uc:uc + 1], None,
                                 op0=OP.mult)
                VE.drain()
                VE.scalar_tensor_tensor(p_l[:, pcol], osl, rden[:, uc:uc + 1],
                                        p_h[:, pcol], op0=OP.mult, op1=OP.subtract)
                v2t.inc(VE.drain())
                v2t.wait(TE)
                bi = i % 2
                tr_free[bi].wait(TE)
                TE.transpose(otr[bi][:, :128], p_h[:, pcol], ident_s[:, :])
                last = TE.transpose(otr[bi][:, 128:256], p_l[:, pcol], ident_s[:, :])
                tr_done[bi].inc(last)
                t2v.inc(TE.drain())
                tr_marks.append(t2v.n)
                eng = VE if bi == 0 else SE
                tr_done[bi].wait(eng)
                dcol = slice(h * 512 + q * 128, h * 512 + (q + 1) * 128)
                if eng is VE:
                    VE.tensor_copy(q_h[:, dcol], otr[bi][:, :128])
                    VE.tensor_copy(q_l[:, dcol], otr[bi][:, 128:256])
                    vc2.inc(VE.drain())
                    tr_free[bi].inc(VE.engine_nop())
                else:
                    SE.copy(q_h[:, dcol], otr[bi][:, :128])
                    SE.copy(q_l[:, dcol], otr[bi][:, 128:256])
                    sc2.inc(SE.drain())
                    tr_free[bi].inc(SE.drain())
        vc2.wait(TE)
        sc2.wait(TE)
        bar()
        otr_ctx.close()
        wctx = ExitStack()
        wqf = wctx.enter_context(nc.sbuf_tensor(f"wof{phase}", [128, 2 * 1024], F32))
        wqh = wctx.enter_context(nc.sbuf_tensor(f"woh{phase}", [128, 2 * 1024], BF16))
        wql = wctx.enter_context(nc.sbuf_tensor(f"wol{phase}", [128, 2 * 1024], BF16))
        wops_ctx = ExitStack()
        wops = [wops_ctx.enter_context(
            nc.psum_tensor(f"wops{phase}_{i}", [128, 512], F32)) for i in range(8)]
        wo_marks = []
        li = 0
        for ocg in range(2):
            for k in range(KCH):
                slot = li % 2
                wsl = wqf[:, slot * 1024:(slot + 1) * 1024]
                if li >= 2:
                    SY.wait_ge(t2v.h, wo_marks[li - 2])
                s_wq[slot].inc(SY.dma_start(
                    out=wsl,
                    in_=d_wo[k * 128:(k + 1) * 128, ocg * 1024:(ocg + 1) * 1024]), 16)
                s_wq[slot].wait(SE)
                hcols = slice(slot * 1024, (slot + 1) * 1024)
                SE.copy(wqh[:, hcols], wsl)
                sc2.inc(SE.drain())
                sc2.wait(VE)
                VE.tensor_tensor(wql[:, hcols], wsl, wqh[:, hcols], op=OP.subtract)
                vc2.inc(VE.drain())
                vc2.wait(TE)
                sc2.wait(TE)
                for tt in range(4):
                    for oc in range(2):
                        pb = wops[tt * 2 + oc]
                        wh = wqh[:, slot * 1024 + oc * 512: slot * 1024 + (oc + 1) * 512]
                        wl = wql[:, slot * 1024 + oc * 512: slot * 1024 + (oc + 1) * 512]
                        lh = q_h[:, k * 512 + tt * 128: k * 512 + (tt + 1) * 128]
                        ll = q_l[:, k * 512 + tt * 128: k * 512 + (tt + 1) * 128]
                        TE.matmul(pb[:, :], lh, wh, start=(k == 0), stop=False)
                        TE.matmul(pb[:, :], ll, wh, start=False, stop=False)
                        last = TE.matmul(pb[:, :], lh, wl, start=False, stop=(k == KCH - 1))
                t2v.inc(last)
                wo_marks.append(t2v.n)
                li += 1
            t2v.wait(VE)
            for tt in range(4):
                slot = tile_ctr[0] % 2
                tile_ctr[0] += 1
                sl = x_sb[:, slot * DM:(slot + 1) * DM]
                xf = s_xf[slot]
                xf.wait(SY)
                r0 = phase * QB + tt * 128
                s_x[slot].inc(SY.dma_start(out=sl[:, :1024],
                                           in_=d_xq[r0:r0 + 128,
                                                    ocg * 1024:(ocg + 1) * 1024]), 16)
                s_x[slot].wait(VE)
                for oc in range(2):
                    VE.tensor_tensor(sl[:, 1024 + oc * 512: 1024 + (oc + 1) * 512],
                                     wops[tt * 2 + oc][:, :],
                                     sl[:, oc * 512:(oc + 1) * 512], op=OP.add)
                vc2.inc(VE.drain())
                v2t.inc(VE.engine_nop())
                vc2.wait(SY)
                st_x[slot].inc(SY.dma_start(
                    out=d_rsp[r0:r0 + 128, ocg * 1024:(ocg + 1) * 1024],
                    in_=sl[:, 1024:2048]), 16)
                st_x[slot].wait(SY)
                xf.inc(SY.nop(), 1)
            v2t.wait(TE)
        bar()
        wops_ctx.close()
        wctx.close()

    for phase, nsteps in ((0, ASTEPS), (1, BSTEPS)):
        VE.memset(m_run[:, :64], NEG)
        VE.memset(sden[:, :64], 0.0)
        vc2.inc(VE.drain())
        q_v, q_s = q_build2(phase)
        if stage_done():
            return nc
        o_ctx = ExitStack()
        o_acc = o_ctx.enter_context(nc.sbuf_tensor(f"o_acc{phase}", [128, 4 * DM], F32))
        for qq in range(4):
            VE.memset(o_acc[:, qq * DM:(qq + 1) * DM], 0.0)
        vc2.inc(VE.drain())
        kv_ctx = ExitStack()
        k_h = kv_ctx.enter_context(nc.sbuf_tensor(f"k_h{phase}", [128, KVH * 512], BF16))
        k_l = kv_ctx.enter_context(nc.sbuf_tensor(f"k_l{phase}", [128, KVH * 512], BF16))
        v_h = kv_ctx.enter_context(nc.sbuf_tensor(f"v_h{phase}", [128, 4 * 512], BF16))
        v_l = kv_ctx.enter_context(nc.sbuf_tensor(f"v_l{phase}", [128, 4 * 512], BF16))
        aps_ctx = ExitStack()
        sc_ps = [aps_ctx.enter_context(nc.psum_tensor(f"sc{phase}_{i}", [128, 512], F32))
                 for i in range(6)]
        pt_ps = [aps_ctx.enter_context(nc.psum_tensor(f"pt{phase}_{i}", [128, 1024], BF16))
                 for i in range(2)]
        pa_ps = [sc_ps[0], sc_ps[1]]
        kv_bufs = (k_h, k_l, v_h, v_l)
        aps = (sc_ps, pt_ps, pa_ps)
        for step in range(nsteps):
            kv_v, kv_s = kv_chunk(phase, step, kv_bufs, aps)
            if not os.environ.get("KB_NO_ATTEND"):
                attend_step(phase, step, kv_bufs, aps, o_acc, kv_v, kv_s, q_v, q_s)
            if not os.environ.get("KB_NO_SCALE"):
                emit_scale_tiles(32)
        if stage_done():
            return nc
        bar()
        aps_ctx.close()
        kv_ctx.close()
        wo_phase(phase, nsteps, o_acc)
        if stage_done():
            return nc
        bar()
        o_ctx.close()
    emit_scale_tiles(384)

    sc2.wait(VE)
    for wi in range(3):
        VE.reduce_sum(asum[:, wi:wi + 1], wacc[:, wi * 128:(wi + 1) * 128], axis=AX)
    VE.drain()
    VE.tensor_copy(wabs_t[:, 0:3], asum[:, 0:3])
    VE.drain()
    VE.tensor_tensor(asum[:, 4:7], asum[:, 0:3], wabs_t[:, 0:3], op=OP.subtract)
    VE.drain()
    VE.tensor_copy(wabs_t[:, 4:7], asum[:, 4:7])
    VE.drain()
    VE.memset(wabs_t[:, 3:4], 0.0)
    VE.memset(wabs_t[:, 7:8], 0.0)
    vc2.inc(VE.drain())
    v2t.inc(VE.engine_nop())
    v2t.wait(TE)
    gp_done.wait(TE)
    scp_ctx = ExitStack()
    scp = scp_ctx.enter_context(nc.psum_tensor("scp", [128, 8], F32))
    last = TE.matmul(scp[:, :], ones_b[:, :], wabs_t[:, :8], start=True, stop=True)
    t2v.inc(last)
    t2v.wait(VE)
    VE.tensor_copy(wsml[:, 8:16], scp[:, 0:8])
    VE.drain()
    VE.tensor_tensor(wsml[:, 0:3], wsml[:, 8:11], wsml[:, 12:15], op=OP.add)
    VE.drain()
    VE.tensor_scalar(wsml[:, 0:3], wsml[:, 0:3], float(2.0 ** -24), EPS,
                     op0=OP.mult, op1=OP.add)
    VE.drain()
    VE.tensor_scalar(wsml[:, 4:7], wsml[:, 0:3], 0.5, None, op0=OP.mult)
    vc2.inc(VE.drain())
    v2t.inc(VE.engine_nop())
    v2t.wait(TE)
    bar()
    scp_ctx.close()
    attn_ctx.close()
    if stage_done():
        return nc

    WSG, WSU, WSD = 0, 1, 2
    ffn_ctx = ExitStack()
    fst = ffn_ctx.enter_context(nc.sbuf_tensor("fst", [128, 48], F32))
    fscr_ctx = ExitStack()
    f32a = fscr_ctx.enter_context(nc.sbuf_tensor("f32a", [128, 2 * DM], F32))
    f32b = fscr_ctx.enter_context(nc.sbuf_tensor("f32b", [128, 2 * DM], F32))
    bf_a = fscr_ctx.enter_context(nc.sbuf_tensor("bf_a", [128, 2 * DM], BF16))
    i32a = fscr_ctx.enter_context(nc.sbuf_tensor("i32a", [128, DM], I32))

    def quant_act_tile(src_f32, rr_ap, dst_bf, W):
        VE.tensor_scalar(i32a[:, :W], src_f32, rr_ap, None, op0=OP.mult)
        VE.drain()
        VE.tensor_copy(f32b[:, :W], i32a[:, :W])
        VE.drain()
        VE.scalar_tensor_tensor(f32b[:, DM:DM + W], src_f32, rr_ap, f32b[:, :W],
                                op0=OP.mult, op1=OP.subtract)
        VE.drain()
        VE.tensor_scalar(bf_a[:, :W], f32b[:, DM:DM + W], 0.5, None, op0=OP.is_ge)
        VE.tensor_scalar(bf_a[:, DM:DM + W], f32b[:, DM:DM + W], -1.0, 0.5,
                         op0=OP.mult, op1=OP.is_ge)
        VE.drain()
        VE.tensor_tensor(f32b[:, DM:DM + W], bf_a[:, :W], bf_a[:, DM:DM + W],
                         op=OP.subtract)
        VE.drain()
        VE.tensor_tensor(dst_bf, f32b[:, :W], f32b[:, DM:DM + W], op=OP.add)
        vc2.inc(VE.drain())

    xq_ctx = ExitStack()
    xqfm = xq_ctx.enter_context(nc.sbuf_tensor("xqfm", [128, KCH * NTOK], BF16))
    f1_ps = ExitStack()
    f1tr = [f1_ps.enter_context(nc.psum_tensor("f1tr0", [128, 1024], BF16)),
            f1_ps.enter_context(nc.psum_tensor("f1tr1", [128, 1024], BF16))]
    f1_marks = []
    f1_tr_marks = []
    for tt in range(8):
        r0 = tt * 128
        slot = tt % 2
        sl = f32a[:, slot * DM:(slot + 1) * DM]
        if tt >= 2:
            SY.wait_ge(vc2.h, f1_marks[tt - 2])
        s_fa[slot].inc(SY.dma_start(out=sl, in_=d_rsp[r0:r0 + 128, :]), 16)
        s_fa[slot].wait(VE)
        for c in range(4):
            VE.bn_stats(bn6[:, 6 * c:6 * (c + 1)], sl[:, 512 * c:512 * (c + 1)])
        VE.drain()
        VE.bn_aggr(bn2[:, :], bn6[:, :])
        VE.drain()
        VE.tensor_scalar(lnsc[:, 0:1], bn2[:, 0:1], -1.0, None, op0=OP.mult)
        vc2.inc(VE.drain())
        vc2.wait(SE)
        SE.activation(lnsc[:, 1:2], bn2[:, 1:2], AF.Sqrt, bias=lnsc[:, 7:8], scale=1.0)
        sc2.inc(SE.drain())
        sc2.wait(VE)
        VE.reciprocal(lnsc[:, 2:3], lnsc[:, 1:2])
        VE.drain()
        VE.tensor_scalar(sl, sl, lnsc[:, 0:1], None, op0=OP.add)
        VE.drain()
        VE.reduce_max(lnsc[:, 4:5], sl, axis=AX, apply_absolute_value=True)
        VE.drain()
        VE.scalar_tensor_tensor(fst[:, tt:tt + 1], lnsc[:, 4:5],
                                float(1.0 / 127.0), lnsc[:, 2:3],
                                op0=OP.mult, op1=OP.mult)
        VE.drain()
        VE.tensor_scalar(fst[:, tt:tt + 1], fst[:, tt:tt + 1], EPS, None,
                         op0=OP.add)
        VE.drain()
        VE.reciprocal(lnsc[:, 4:5], fst[:, tt:tt + 1])
        VE.drain()
        VE.tensor_tensor(lnsc[:, 4:5], lnsc[:, 4:5], lnsc[:, 2:3], op=OP.mult)
        VE.drain()
        if tt >= 1:
            VE.wait_ge(t2v.h, f1_tr_marks[tt - 1])
        quant_act_tile(sl, lnsc[:, 4:5], bf_a[:, :DM], DM)
        f1_marks.append(vc2.n)
        vc2.wait(TE)
        done = 0
        while done < KCH:
            bi = (done // 8) % 2
            take = min(8, KCH - done)
            tr_free[bi].wait(TE)
            last = None
            for i in range(take):
                last = TE.transpose(f1tr[bi][:, i * 128:(i + 1) * 128],
                                    bf_a[:, (done + i) * 128:(done + i + 1) * 128],
                                    ident_s[:, :])
            tr_done[bi].inc(last)
            eng = VE if bi == 0 else SE
            tr_done[bi].wait(eng)
            for i in range(take):
                col = (done + i) * NTOK + tt * 128
                if eng is VE:
                    VE.tensor_copy(xqfm[:, col:col + 128],
                                   f1tr[bi][:, i * 128:(i + 1) * 128])
                else:
                    SE.copy(xqfm[:, col:col + 128],
                            f1tr[bi][:, i * 128:(i + 1) * 128])
            if eng is VE:
                vc2.inc(VE.drain())
                tr_free[bi].inc(VE.engine_nop())
            else:
                sc2.inc(SE.drain())
                tr_free[bi].inc(SE.drain())
            done += take
        t2v.inc(TE.drain())
        f1_tr_marks.append(t2v.n)
    vc2.wait(VE)
    VE.tensor_scalar(fst[:, 16:24], fst[:, 0:8], wsml[:, 0:1], None, op0=OP.mult)
    VE.tensor_scalar(fst[:, 24:32], fst[:, 0:8], wsml[:, 1:2], None, op0=OP.mult)
    VE.memset(fst[:, 32:40], 0.0)
    vc2.inc(VE.drain())
    if stage_done():
        return nc
    bar()
    f1_ps.close()

    f2_ctx = ExitStack()
    wgq = f2_ctx.enter_context(nc.sbuf_tensor("wgq", [128, KCH * 512], BF16))
    wuq = f2_ctx.enter_context(nc.sbuf_tensor("wuq", [128, KCH * 512], BF16))
    g_sb = f2_ctx.enter_context(nc.sbuf_tensor("g_sb", [128, 8 * 512], F32))
    h_sb = f2_ctx.enter_context(nc.sbuf_tensor("h_sb", [128, 2 * 512], F32))
    f2_ps = ExitStack()
    fps = [f2_ps.enter_context(nc.psum_tensor(f"fps{i}", [128, 512], F32))
           for i in range(8)]
    fps_free = [S(nc, f"fpsf{i}") for i in range(8)]
    fps_done = [0] * 8
    fq_marks = []
    fq_li = [0]
    h_st_marks = []

    def quant_w_tiles(dsrc, dst, wi, oc, mm_guard):
        for k in range(KCH):
            li = fq_li[0]
            fq_li[0] += 1
            slot = li % 2
            sl = f32a[:, slot * DM: slot * DM + 512]
            if li >= 2:
                SY.wait_ge(vc2.h, fq_marks[li - 2])
            s_fa[slot].inc(SY.dma_start(
                out=sl, in_=dsrc[k * 128:(k + 1) * 128,
                                 oc * 512:(oc + 1) * 512]), 16)
            s_fa[slot].wait(VE)
            if k == 0:
                VE.wait_ge(t2v.h, mm_guard)
            bslot = li % 2
            bc2 = slice(bslot * 1024 + 512, bslot * 1024 + 1024)
            VE.tensor_scalar(bf_a[:, bc2], sl, -1.0, wsml[:, 4 + wi:5 + wi],
                             op0=OP.mult, op1=OP.is_ge)
            VE.drain()
            vc2.inc(VE.scalar_tensor_tensor(dst[:, k * 512:(k + 1) * 512], sl,
                                            wsml[:, 4 + wi:5 + wi], bf_a[:, bc2],
                                            op0=OP.is_ge, op1=OP.subtract))
            fq_marks.append(vc2.n)

    mm_guard = [t2v.n]
    for oc in range(FI // 512):
        quant_w_tiles(d_wgT, wgq, WSG, oc, mm_guard[0])
        quant_w_tiles(d_wuT, wuq, WSU, oc, mm_guard[0])
        vc2.wait(TE)
        for tt in range(8):
            pb = fps[tt % 4]
            fps_free[tt % 4].wait(TE)
            for k in range(KCH):
                last = TE.matmul(pb[:, :],
                                 xqfm[:, k * NTOK + tt * 128: k * NTOK + (tt + 1) * 128],
                                 wgq[:, k * 512:(k + 1) * 512],
                                 start=(k == 0), stop=(k == KCH - 1))
            t2v.inc(last)
            fps_done[tt % 4] = t2v.n
            SE.wait_ge(t2v.h, fps_done[tt % 4])
            SE.activation(g_sb[:, tt * 512:(tt + 1) * 512], pb[:, :], AF.Silu,
                          scale=fst[:, 16 + tt:17 + tt])
            sc2.inc(SE.drain())
            fps_free[tt % 4].inc(SE.drain())
        for tt in range(8):
            pb = fps[4 + tt % 4]
            fps_free[4 + tt % 4].wait(TE)
            for k in range(KCH):
                last = TE.matmul(pb[:, :],
                                 xqfm[:, k * NTOK + tt * 128: k * NTOK + (tt + 1) * 128],
                                 wuq[:, k * 512:(k + 1) * 512],
                                 start=(k == 0), stop=(k == KCH - 1))
            t2v.inc(last)
            fps_done[4 + tt % 4] = t2v.n
            hidx = oc * 8 + tt
            hs = hidx % 2
            hsl = h_sb[:, hs * 512:(hs + 1) * 512]
            VE.wait_ge(t2v.h, fps_done[4 + tt % 4])
            sc2.wait(VE)
            if hidx >= 2:
                st_h[hs].wait(VE)
            VE.scalar_tensor_tensor(hsl, pb[:, :], fst[:, 24 + tt:25 + tt],
                                    g_sb[:, tt * 512:(tt + 1) * 512],
                                    op0=OP.mult, op1=OP.mult)
            VE.drain()
            fps_free[4 + tt % 4].inc(VE.engine_nop())
            VE.reduce_max(lnsc[:, 5:6], hsl, axis=AX, apply_absolute_value=True)
            VE.drain()
            VE.tensor_tensor(fst[:, 32 + tt:33 + tt], fst[:, 32 + tt:33 + tt],
                             lnsc[:, 5:6], op=OP.max)
            vc2.inc(VE.drain())
            vc2.wait(SY)
            st_h[hs].inc(SY.dma_start(
                out=d_hsp[tt * 128:(tt + 1) * 128, oc * 512:(oc + 1) * 512],
                in_=hsl), 16)
        mm_guard[0] = t2v.n
    vc2.wait(VE)
    VE.tensor_scalar(fst[:, 32:40], fst[:, 32:40], float(1.0 / 127.0), EPS,
                     op0=OP.mult, op1=OP.add)
    VE.drain()
    VE.reciprocal(fst[:, 40:48], fst[:, 32:40])
    VE.drain()
    VE.tensor_scalar(fst[:, 32:40], fst[:, 32:40], wsml[:, 2:3], None, op0=OP.mult)
    vc2.inc(VE.drain())
    st_h[0].wait(SY)
    st_h[1].wait(SY)
    if stage_done():
        return nc
    bar()
    f2_ps.close()
    f2_ctx.close()
    xq_ctx.close()

    f3_ctx = ExitStack()
    hqfm = f3_ctx.enter_context(nc.sbuf_tensor("hqfm", [128, 64 * NTOK], BF16))
    f3_ps = ExitStack()
    f3tr = [f3_ps.enter_context(nc.psum_tensor("f3tr0", [128, 1024], BF16)),
            f3_ps.enter_context(nc.psum_tensor("f3tr1", [128, 1024], BF16))]
    f3_marks = []
    f3_tr_marks = []
    for tt in range(8):
        for pc in range(4):
            li = tt * 4 + pc
            slot = li % 2
            sl = f32a[:, slot * DM:(slot + 1) * DM]
            if li >= 2:
                SY.wait_ge(vc2.h, f3_marks[li - 2])
            s_fa[slot].inc(SY.dma_start(
                out=sl, in_=d_hsp[tt * 128:(tt + 1) * 128,
                                  pc * DM:(pc + 1) * DM]), 16)
            s_fa[slot].wait(VE)
            if li >= 1:
                VE.wait_ge(t2v.h, f3_tr_marks[li - 1])
            quant_act_tile(sl, fst[:, 40 + tt:41 + tt], bf_a[:, :DM], DM)
            f3_marks.append(vc2.n)
            vc2.wait(TE)
            done = 0
            while done < KCH:
                bi = (done // 8) % 2
                take = min(8, KCH - done)
                tr_free[bi].wait(TE)
                last = None
                for i in range(take):
                    last = TE.transpose(f3tr[bi][:, i * 128:(i + 1) * 128],
                                        bf_a[:, (done + i) * 128:(done + i + 1) * 128],
                                        ident_s[:, :])
                tr_done[bi].inc(last)
                eng = VE if bi == 0 else SE
                tr_done[bi].wait(eng)
                for i in range(take):
                    ic = pc * KCH + done + i
                    col = ic * NTOK + tt * 128
                    if eng is VE:
                        VE.tensor_copy(hqfm[:, col:col + 128],
                                       f3tr[bi][:, i * 128:(i + 1) * 128])
                    else:
                        SE.copy(hqfm[:, col:col + 128],
                                f3tr[bi][:, i * 128:(i + 1) * 128])
                if eng is VE:
                    vc2.inc(VE.drain())
                    tr_free[bi].inc(VE.engine_nop())
                else:
                    sc2.inc(SE.drain())
                    tr_free[bi].inc(SE.drain())
                done += take
            t2v.inc(TE.drain())
            f3_tr_marks.append(t2v.n)
    if stage_done():
        return nc
    bar()
    f3_ps.close()

    f4s_ctx = ExitStack()
    bf16_wd = f4s_ctx.enter_context(nc.sbuf_tensor("bf16_wd", [128, 2 * 512], BF16))
    f4_ps = ExitStack()
    dps = [f4_ps.enter_context(nc.psum_tensor(f"dps{i}", [128, 512], F32))
           for i in range(8)]
    dps_free = [S(nc, f"dpsf{i}") for i in range(8)]
    f4_marks = []
    f4_marks_mm = []
    f4_li = [0]
    for ocd in range(4):
        for ic in range(64):
            li = f4_li[0]
            f4_li[0] += 1
            slot = li % 2
            sl = f32a[:, slot * DM: slot * DM + 512]
            if li >= 2:
                SY.wait_ge(vc2.h, f4_marks[li - 2])
            s_fa[slot].inc(SY.dma_start(
                out=sl, in_=d_wdT[ic * 128:(ic + 1) * 128,
                                  ocd * 512:(ocd + 1) * 512]), 16)
            s_fa[slot].wait(VE)
            bslot = li % 2
            bc2 = slice(bslot * 1024 + 512, bslot * 1024 + 1024)
            VE.tensor_scalar(bf_a[:, bc2], sl, -1.0, wsml[:, 6:7],
                             op0=OP.mult, op1=OP.is_ge)
            VE.drain()
            if li >= 2:
                VE.wait_ge(t2v.h, f4_marks_mm[li - 2])
            vc2.inc(VE.scalar_tensor_tensor(bf16_wd[:, bslot * 512:(bslot + 1) * 512],
                                            sl, wsml[:, 6:7], bf_a[:, bc2],
                                            op0=OP.is_ge, op1=OP.subtract))
            f4_marks.append(vc2.n)
            vc2.wait(TE)
            if ic == 0:
                for tt in range(8):
                    dps_free[tt].wait(TE)
            last = None
            for tt in range(8):
                last = TE.matmul(dps[tt][:, :],
                                 hqfm[:, ic * NTOK + tt * 128: ic * NTOK + (tt + 1) * 128],
                                 bf16_wd[:, bslot * 512:(bslot + 1) * 512],
                                 start=(ic == 0), stop=(ic == 63))
            t2v.inc(last)
            f4_marks_mm.append(t2v.n)
        t2v.wait(VE)
        for tt in range(8):
            slot = tile_ctr[0] % 2
            tile_ctr[0] += 1
            sl = x_sb[:, slot * DM:(slot + 1) * DM]
            xf = s_xf[slot]
            xf.wait(SY)
            r0 = tt * 128
            s_x[slot].inc(SY.dma_start(out=sl[:, :512],
                                       in_=d_rsp[r0:r0 + 128,
                                                 ocd * 512:(ocd + 1) * 512]), 16)
            s_x[slot].wait(VE)
            VE.scalar_tensor_tensor(sl[:, 512:1024], dps[tt][:, :],
                                    fst[:, 32 + tt:33 + tt], sl[:, :512],
                                    op0=OP.mult, op1=OP.add)
            vc2.inc(VE.drain())
            dps_free[tt].inc(VE.engine_nop())
            vc2.wait(SY)
            st_x[slot].inc(SY.dma_start(out=d_out[r0:r0 + 128,
                                                  ocd * 512:(ocd + 1) * 512],
                                        in_=sl[:, 512:1024]), 16)
            st_x[slot].wait(SY)
            xf.inc(SY.nop(), 1)
    bar()
    f4_ps.close()
    f4s_ctx.close()
    f3_ctx.close()
    fscr_ctx.close()
    ffn_ctx.close()

    for sem in (st_x[0], st_x[1], st_h[0], st_h[1], s_x[0], s_x[1],
                s_wab[0], s_wab[1], s_wq[0], s_wq[1], s_fa[0], s_fa[1], s_cst):
        sem.wait(SY)
    ctx.close()
    return nc


def make_in_maps(L0_attn_x, L0_attn_wq, L0_attn_wk, L0_attn_wv, L0_attn_wo,
                 L0_ffn_w_gate, L0_ffn_w_up, L0_ffn_w_down):
    import ml_dtypes
    x = np.ascontiguousarray(np.asarray(L0_attn_x, dtype=np.float32))
    wq = np.ascontiguousarray(np.asarray(L0_attn_wq, dtype=np.float32))
    wk = np.ascontiguousarray(np.asarray(L0_attn_wk, dtype=np.float32))
    wv = np.ascontiguousarray(np.asarray(L0_attn_wv, dtype=np.float32))
    wo = np.ascontiguousarray(np.asarray(L0_attn_wo, dtype=np.float32))
    wgT = np.ascontiguousarray(np.asarray(L0_ffn_w_gate, dtype=np.float32).T)
    wuT = np.ascontiguousarray(np.asarray(L0_ffn_w_up, dtype=np.float32).T)
    wdT = np.ascontiguousarray(np.asarray(L0_ffn_w_down, dtype=np.float32).T)

    ident = np.eye(128, dtype=ml_dtypes.bfloat16)
    masks = np.zeros((4 * 128, CH), dtype=np.float32)
    for t in range(4):
        for i in range(128):
            masks[t * 128 + i, t * 128 + i + 1:] = NEG
    masks = masks.astype(ml_dtypes.bfloat16)

    in_maps = []
    for c in range(8):
        b, j = c // 4, c % 4
        blkA, blkB = j, 7 - j
        xq = np.concatenate([x[b, blkA * QB:(blkA + 1) * QB],
                             x[b, blkB * QB:(blkB + 1) * QB]], axis=0)
        xkv = np.zeros((STEPS * CH, DM), dtype=np.float32)
        xkv[0:CH] = x[b, blkA * QB:(blkA + 1) * QB]
        for s in range(1, blkA + 1):
            xkv[s * CH:(s + 1) * CH] = x[b, (s - 1) * CH:s * CH]
        ob = ASTEPS * CH
        xkv[ob:ob + CH] = x[b, blkB * QB:(blkB + 1) * QB]
        for s in range(1, blkB + 1):
            xkv[ob + s * CH: ob + (s + 1) * CH] = x[b, (s - 1) * CH:s * CH]
        gates = np.zeros((128, STEPS), dtype=np.float32)
        for s in range(ASTEPS):
            if s > blkA:
                gates[:, s] = NEG
        for s in range(BSTEPS):
            if s > blkB:
                gates[:, ASTEPS + s] = NEG
        in_maps.append({
            "xq_rows": np.ascontiguousarray(xq),
            "x_kv": xkv,
            "wq": wq, "wk": wk, "wv": wv, "wo": wo,
            "wgT": wgT, "wuT": wuT, "wdT": wdT,
            "masks": masks, "gates": gates, "ident": ident,
        })

    return in_maps


def assemble_out(results):
    out = np.empty((B, T, DM), dtype=np.float32)
    for c in range(8):
        b, j = c // 4, c % 4
        blkA, blkB = j, 7 - j
        o = results[c]["out"]
        out[b, blkA * QB:(blkA + 1) * QB] = o[:QB]
        out[b, blkB * QB:(blkB + 1) * QB] = o[QB:]
    return out


def kernel(**inputs):
    in_maps = make_in_maps(**inputs)
    nc = build_nc()
    res = run_bass_kernel_spmd(nc, in_maps, core_ids=list(range(8)))
    return assemble_out(res.results)


# version 13
# speedup vs baseline: 1.0069x; 1.0069x over previous
import os
import numpy as np
from contextlib import ExitStack

import concourse.bass as bass
import concourse.mybir as mybir
from concourse.bass_utils import run_bass_kernel_spmd
from concourse.alu_op_type import AluOpType as OP

F32 = mybir.dt.float32
BF16 = mybir.dt.bfloat16
I32 = mybir.dt.int32
AF = mybir.ActivationFunctionType
AX = mybir.AxisListType.X

B, T, DM = 2, 4096, 2048
H, KVH, HD = 16, 4, 128
NREP = H // KVH
QB = 512
CH = 512
ASTEPS, BSTEPS = 4, 8
STEPS = ASTEPS + BSTEPS
KCH = DM // 128
SCALE = float(1.0 / np.sqrt(np.float32(HD)))
NEG = -1.0e30
EPS = 1e-5
FI = 8192
NTOK = 1024


class S:
    def __init__(self, nc, name):
        self.h = nc._ctx.enter_context(nc.semaphore(name))
        self.n = 0

    def inc(self, instr, k=1):
        instr.then_inc(self.h, k)
        self.n += k
        return instr

    def wait(self, eng, n=None):
        eng.wait_ge(self.h, self.n if n is None else n)


def build_nc():
    nc = bass.Bass("TRN2", target_bir_lowering=False, debug=False)
    ctx = ExitStack()
    nc._ctx = ctx

    d_xq = nc.dram_tensor("xq_rows", [NTOK, DM], F32, kind="ExternalInput").ap()
    d_xkv = nc.dram_tensor("x_kv", [STEPS * CH, DM], F32, kind="ExternalInput").ap()
    d_wq = nc.dram_tensor("wq", [DM, DM], F32, kind="ExternalInput").ap()
    d_wk = nc.dram_tensor("wk", [DM, KVH * HD], F32, kind="ExternalInput").ap()
    d_wv = nc.dram_tensor("wv", [DM, KVH * HD], F32, kind="ExternalInput").ap()
    d_wo = nc.dram_tensor("wo", [DM, DM], F32, kind="ExternalInput").ap()
    d_wgT = nc.dram_tensor("wgT", [DM, FI], F32, kind="ExternalInput").ap()
    d_wuT = nc.dram_tensor("wuT", [DM, FI], F32, kind="ExternalInput").ap()
    d_wdT = nc.dram_tensor("wdT", [FI, DM], F32, kind="ExternalInput").ap()
    d_masks = nc.dram_tensor("masks", [4 * 128, CH], BF16, kind="ExternalInput").ap()
    d_gates = nc.dram_tensor("gates", [128, STEPS], F32, kind="ExternalInput").ap()
    d_ident = nc.dram_tensor("ident", [128, 128], BF16, kind="ExternalInput").ap()
    d_out = nc.dram_tensor("out", [NTOK, DM], F32, kind="ExternalOutput").ap()
    d_hsp = nc.dram_tensor("h_spill", [NTOK, FI], F32, kind="Internal").ap()
    d_rsp = nc.dram_tensor("res_spill", [NTOK, DM], F32, kind="Internal").ap()

    TE, VE, SE, SY, GP = nc.tensor, nc.vector, nc.scalar, nc.sync, nc.gpsimd

    s_cst = S(nc, "s_cst")
    s_x = [S(nc, "s_x0"), S(nc, "s_x1")]
    s_wab = [S(nc, "s_wab0"), S(nc, "s_wab1")]
    s_wq = [S(nc, "s_wq0"), S(nc, "s_wq1")]
    s_fa = [S(nc, "s_fa0"), S(nc, "s_fa1")]
    st_h = [S(nc, "st_h0"), S(nc, "st_h1")]
    st_x = [S(nc, "st_x0"), S(nc, "st_x1")]
    s_xf = [S(nc, "s_xf0"), S(nc, "s_xf1")]
    t2v = S(nc, "t2v")
    v2t = S(nc, "v2t")
    sc2 = S(nc, "sc2")
    vc2 = S(nc, "vc2")
    gp_done = S(nc, "gp_done")
    a_sc = [S(nc, "a_sc0"), S(nc, "a_sc1"), S(nc, "a_sc2")]
    a_bf = [S(nc, "a_bf0"), S(nc, "a_bf1"), S(nc, "a_bf2")]
    a_vb = S(nc, "a_vb")
    a_sp = S(nc, "a_sp")
    a_ph = S(nc, "a_ph")
    a_pl = S(nc, "a_pl")
    a_pt = [S(nc, "a_pt0"), S(nc, "a_pt1"), S(nc, "a_pt2")]
    a_pe = [S(nc, "a_pe0"), S(nc, "a_pe1"), S(nc, "a_pe2")]
    spl_v = S(nc, "spl_v")
    spl_s = S(nc, "spl_s")
    tr_done = [S(nc, "trd0"), S(nc, "trd1")]
    tr_free = [S(nc, "trf0"), S(nc, "trf1")]
    pa_done = [S(nc, "pad0"), S(nc, "pad1")]
    pa_free = [S(nc, "paf0"), S(nc, "paf1")]

    x_sb = ctx.enter_context(nc.sbuf_tensor("x_sb", [128, 2 * DM], F32))
    ident_s = ctx.enter_context(nc.sbuf_tensor("ident_s", [128, 128], BF16))
    gates_s = ctx.enter_context(nc.sbuf_tensor("gates_s", [128, STEPS], F32))
    ones_b = ctx.enter_context(nc.sbuf_tensor("ones_b", [128, 128], BF16))
    bn6 = ctx.enter_context(nc.sbuf_tensor("bn6", [128, 24], F32))
    bn2 = ctx.enter_context(nc.sbuf_tensor("bn2", [128, 2], F32))
    lnsc = ctx.enter_context(nc.sbuf_tensor("lnsc", [128, 8], F32))
    wabs = ctx.enter_context(nc.sbuf_tensor("wabs", [128, 2 * 1024], F32))
    asum = ctx.enter_context(nc.sbuf_tensor("asum", [128, 8], F32))
    wsml = ctx.enter_context(nc.sbuf_tensor("wsml", [128, 16], F32))
    wabs_t = ctx.enter_context(nc.sbuf_tensor("wabs_t", [128, 8], BF16))
    wacc = ctx.enter_context(nc.sbuf_tensor("wacc", [128, 384], F32))

    tile_ctr = [0]
    unit_ctr = [0]
    pbar = S(nc, "pbar")

    def bar():
        for eng in (TE, VE, SE, GP, SY):
            pbar.inc(eng.drain())
        for eng in (TE, VE, SE, GP, SY):
            pbar.wait(eng)

    STAGE_LIMIT = int(os.environ.get("KB_STAGE", "99"))
    stage_ctr = [0]

    def stage_done():
        stage_ctr[0] += 1
        if stage_ctr[0] < STAGE_LIMIT:
            return False
        bar()
        fin = S(nc, "fin")
        fin.inc(SY.dma_start(out=d_out[0:128, :], in_=x_sb[:, 0:DM]), 16)
        fin.wait(SY)
        return True


    def ln1_subtile(src_rows, banks, done_s, free_s, ln_h, ln_l, emit_dst):
        slot = tile_ctr[0] % 2
        tile_ctr[0] += 1
        sl = x_sb[:, slot * DM:(slot + 1) * DM]
        xf = s_xf[slot]
        xf.wait(SY)
        s_x[slot].inc(SY.dma_start(out=sl, in_=src_rows), 16)
        s_x[slot].wait(VE)
        for c in range(4):
            VE.bn_stats(bn6[:, 6 * c:6 * (c + 1)], sl[:, 512 * c:512 * (c + 1)])
        VE.drain()
        VE.bn_aggr(bn2[:, :], bn6[:, :])
        VE.drain()
        vc2.inc(VE.tensor_scalar(lnsc[:, 0:1], bn2[:, 0:1], -1.0, None, op0=OP.mult))
        vc2.wait(SE)
        sc2.inc(SE.activation(lnsc[:, 1:2], bn2[:, 1:2], AF.Sqrt, bias=lnsc[:, 7:8], scale=1.0))
        sc2.wait(VE)
        VE.reciprocal(lnsc[:, 2:3], lnsc[:, 1:2])
        VE.drain()
        spl_v.inc(VE.tensor_scalar(sl, sl, lnsc[:, 0:1], lnsc[:, 2:3], op0=OP.add, op1=OP.mult))
        released = False
        nb = len(banks)
        for half in range(2):
            hsl = sl[:, half * 1024:(half + 1) * 1024]
            spl_v.wait(SE)
            spl_s.inc(SE.copy(ln_h[:, :], hsl))
            spl_s.wait(VE)
            spl_v.inc(VE.tensor_tensor(ln_l[:, :], hsl, ln_h[:, :], op=OP.subtract))
            if half == 1:
                xf.inc(VE.engine_nop(), 1)
                released = True
            for is_lo, srcb in ((0, ln_h), (1, ln_l)):
                bi = (2 * half + is_lo) % nb
                free_s[bi].wait(TE)
                if is_lo == 0:
                    spl_s.wait(TE)
                else:
                    spl_v.wait(TE)
                last = None
                for i in range(8):
                    last = TE.transpose(banks[bi][:, i * 128:(i + 1) * 128],
                                        srcb[:, i * 128:(i + 1) * 128], ident_s[:, :])
                done_s[bi].inc(last)
                eng = VE if (2 * half + is_lo) % 2 == 0 else SE
                done_s[bi].wait(eng)
                emit_dst(eng, banks[bi], half * 8, 8, is_lo)
                if eng is VE:
                    spl_v.inc(VE.engine_nop())
                    free_s[bi].inc(VE.engine_nop())
                else:
                    spl_s.inc(SE.drain())
                    free_s[bi].inc(SE.drain())
        assert released

    attn_ctx = ExitStack()
    wk_h = attn_ctx.enter_context(nc.sbuf_tensor("wk_h", [128, KCH * 512], BF16))
    wk_l = attn_ctx.enter_context(nc.sbuf_tensor("wk_l", [128, KCH * 512], BF16))
    wv_h = attn_ctx.enter_context(nc.sbuf_tensor("wv_h", [128, KCH * 512], BF16))
    wv_l = attn_ctx.enter_context(nc.sbuf_tensor("wv_l", [128, KCH * 512], BF16))
    q_h = attn_ctx.enter_context(nc.sbuf_tensor("q_h", [128, H * 512], BF16))
    q_l = attn_ctx.enter_context(nc.sbuf_tensor("q_l", [128, H * 512], BF16))
    ln_h = attn_ctx.enter_context(nc.sbuf_tensor("ln_h", [128, 1024], BF16))
    ln_l = attn_ctx.enter_context(nc.sbuf_tensor("ln_l", [128, 1024], BF16))
    lnf_h = attn_ctx.enter_context(nc.sbuf_tensor("lnf_h", [128, DM], BF16))
    lnf_l = attn_ctx.enter_context(nc.sbuf_tensor("lnf_l", [128, DM], BF16))
    masks_s = attn_ctx.enter_context(nc.sbuf_tensor("masks_s", [128, 4 * CH], BF16))
    p_f = attn_ctx.enter_context(nc.sbuf_tensor("p_f", [128, 3 * CH], F32))
    p_h = attn_ctx.enter_context(nc.sbuf_tensor("p_h", [128, 3 * CH], BF16))
    p_l = attn_ctx.enter_context(nc.sbuf_tensor("p_l", [128, 3 * CH], BF16))
    pt_hl = attn_ctx.enter_context(nc.sbuf_tensor("pt_hl", [128, 6 * CH], BF16))
    m_run = attn_ctx.enter_context(nc.sbuf_tensor("m_run", [128, 128], F32))
    sden = attn_ctx.enter_context(nc.sbuf_tensor("sden", [128, 128], F32))
    cmx = attn_ctx.enter_context(nc.sbuf_tensor("cmx", [128, 64], F32))
    alph = attn_ctx.enter_context(nc.sbuf_tensor("alph", [128, 64], F32))
    csum = attn_ctx.enter_context(nc.sbuf_tensor("csum", [128, 64], F32))
    bi_p = attn_ctx.enter_context(nc.sbuf_tensor("bi_p", [128, 64], F32))
    bi_a = attn_ctx.enter_context(nc.sbuf_tensor("bi_a", [128, 64], F32))
    rden = attn_ctx.enter_context(nc.sbuf_tensor("rden", [128, 64], F32))

    s_cst.inc(SY.dma_start(out=ident_s[:, :], in_=d_ident), 16)
    s_cst.inc(SY.dma_start(out=gates_s[:, :], in_=d_gates), 16)
    for t in range(4):
        s_cst.inc(SY.dma_start(out=masks_s[:, t * CH:(t + 1) * CH],
                               in_=d_masks[t * 128:(t + 1) * 128, :]), 16)
    s_cst.wait(TE)
    s_cst.wait(VE)
    s_cst.wait(SE)
    GP.memset(ones_b[:, :], 1.0)
    gp_done.inc(GP.engine_nop())
    VE.memset(asum[:, :], 0.0)
    VE.memset(lnsc[:, 7:8], EPS)
    vc2.inc(VE.drain())

    wkv_marks = []
    idx = 0
    for dsrc, dsth, dstl in ((d_wk, wk_h, wk_l), (d_wv, wv_h, wv_l)):
        for k in range(KCH):
            slot = idx % 2
            sl = wabs[:, slot * 1024:slot * 1024 + 512]
            if idx >= 2:
                SY.wait_ge(vc2.h, wkv_marks[idx - 2])
            s_wab[slot].inc(SY.dma_start(out=sl, in_=dsrc[k * 128:(k + 1) * 128, :]), 16)
            s_wab[slot].wait(SE)
            cols = slice(k * 512, (k + 1) * 512)
            sc2.inc(SE.copy(dsth[:, cols], sl))
            sc2.wait(VE)
            vc2.inc(VE.tensor_tensor(dstl[:, cols], sl, dsth[:, cols], op=OP.subtract))
            wkv_marks.append(vc2.n)
            idx += 1
    post_wkv_v = vc2.n
    if stage_done():
        return nc

    scale_tiles = []
    for wi, d in enumerate((d_wgT, d_wuT, d_wdT)):
        rows, cols = (DM, FI) if wi < 2 else (FI, DM)
        for r in range(rows // 128):
            for c in range(cols // 1024):
                scale_tiles.append((wi, d[r * 128:(r + 1) * 128,
                                          c * 1024:(c + 1) * 1024]))
    assert len(scale_tiles) == 384
    sp_state = {"i": 0, "marks": [], "vmarks": []}

    def emit_scale_tiles(n):
        for _ in range(n):
            i = sp_state["i"]
            if i >= len(scale_tiles):
                return
            wi, src = scale_tiles[i]
            slot = i % 2
            sl = wabs[:, slot * 1024:(slot + 1) * 1024]
            if i >= 2:
                SE.wait_ge(sc2.h, sp_state["marks"][i - 2])
            else:
                SE.wait_ge(vc2.h, post_wkv_v)
            s_wab[slot].inc(SE.dma_start(out=sl, in_=src), 16)
            s_wab[slot].wait(SE)
            sc2.inc(SE.activation(sl, sl, AF.Abs, accum_out=wacc[:, i:i + 1]))
            sp_state["marks"].append(sc2.n)
            sp_state["i"] += 1

    def q_build2(phase):
        qctx = ExitStack()
        lnq_h = qctx.enter_context(nc.sbuf_tensor(f"lnq_h{phase}", [128, KCH * 512], BF16))
        lnq_l = qctx.enter_context(nc.sbuf_tensor(f"lnq_l{phase}", [128, KCH * 512], BF16))
        qps_ctx = ExitStack()
        qtr = [qps_ctx.enter_context(
            nc.psum_tensor(f"qtr{phase}_{i}", [128, 1024], BF16)) for i in range(2)]
        for sub in range(4):
            r0 = phase * QB + sub * 128

            def emitq(eng, bank, kbase, take, is_lo, sub=sub):
                dstb = lnq_l if is_lo else lnq_h
                for i in range(take):
                    col = (kbase + i) * 512 + sub * 128
                    if eng is VE:
                        VE.tensor_copy(dstb[:, col:col + 128],
                                       bank[:, i * 128:(i + 1) * 128])
                    else:
                        SE.copy(dstb[:, col:col + 128],
                                bank[:, i * 128:(i + 1) * 128])
            ln1_subtile(d_xq[r0:r0 + 128, :], qtr, [tr_done[0], tr_done[1]],
                        [tr_free[0], tr_free[1]], ln_h, ln_l, emitq)
        bar()
        qps_ctx.close()
        wctx = ExitStack()
        wqf = wctx.enter_context(nc.sbuf_tensor(f"wqf{phase}", [128, 2 * 768], F32))
        wqh = wctx.enter_context(nc.sbuf_tensor(f"wqh{phase}", [128, 2 * 768], BF16))
        wql = wctx.enter_context(nc.sbuf_tensor(f"wql{phase}", [128, 2 * 768], BF16))
        qps_ctx2 = ExitStack()
        qps = [qps_ctx2.enter_context(
            nc.psum_tensor(f"qps{phase}_{i}", [128, 512], F32)) for i in range(6)]
        spl_v.wait(TE)
        spl_s.wait(TE)
        li = 0
        wq_marks = []
        heads_done = 0
        while heads_done < H:
            nh = min(6, H - heads_done)
            for k in range(KCH):
                slot = li % 2
                wsl = wqf[:, slot * 768:slot * 768 + nh * 128]
                if li >= 2:
                    SY.wait_ge(t2v.h, wq_marks[li - 2])
                s_wq[slot].inc(SY.dma_start(
                    out=wsl,
                    in_=d_wq[k * 128:(k + 1) * 128,
                             heads_done * 128:(heads_done + nh) * 128]), 16)
                s_wq[slot].wait(SE)
                hcols = slice(slot * 768, slot * 768 + nh * 128)
                spl_s.inc(SE.copy(wqh[:, hcols], wsl))
                spl_s.wait(VE)
                spl_v.inc(VE.tensor_tensor(wql[:, hcols], wsl, wqh[:, hcols], op=OP.subtract))
                spl_v.wait(TE)
                spl_s.wait(TE)
                for hh in range(nh):
                    hc = slot * 768 + hh * 128
                    wh = wqh[:, hc:hc + 128]
                    wl = wql[:, hc:hc + 128]
                    TE.matmul(qps[hh][:, :], wh, lnq_h[:, k * 512:(k + 1) * 512],
                              start=(k == 0), stop=False)
                    TE.matmul(qps[hh][:, :], wh, lnq_l[:, k * 512:(k + 1) * 512],
                              start=False, stop=False)
                    last = TE.matmul(qps[hh][:, :], wl, lnq_h[:, k * 512:(k + 1) * 512],
                                     start=False, stop=(k == KCH - 1))
                t2v.inc(last)
                wq_marks.append(t2v.n)
                li += 1
            t2v.wait(SE)
            last = None
            for hh in range(nh):
                cols = slice((heads_done + hh) * 512, (heads_done + hh + 1) * 512)
                last = SE.copy(q_h[:, cols], qps[hh][:, :])
            spl_s.inc(last)
            spl_s.wait(VE)
            t2v.wait(VE)
            last = None
            for hh in range(nh):
                cols = slice((heads_done + hh) * 512, (heads_done + hh + 1) * 512)
                last = VE.tensor_tensor(q_l[:, cols], qps[hh][:, :], q_h[:, cols],
                                        op=OP.subtract)
            spl_v.inc(last)
            v2t.inc(VE.engine_nop())
            v2t.wait(TE)
            heads_done += nh
        bar()
        qps_ctx2.close()
        wctx.close()
        qctx.close()
        return spl_v.n, spl_s.n

    def kv_chunk(phase, step, kv_bufs, aps):
        (k_h, k_l, v_h, v_l) = kv_bufs
        (sc_ps, pt_ps, pa_ps) = aps
        base = (0 if phase == 0 else ASTEPS) + step
        TE.wait_ge(vc2.h, vc2.n)
        for tt in range(4):
            r0 = base * CH + tt * 128

            def emitk(eng, bank, kbase, take, is_lo):
                dstb = lnf_l if is_lo else lnf_h
                if eng is VE:
                    VE.tensor_copy(dstb[:, kbase * 128:(kbase + take) * 128],
                                   bank[:, :take * 128])
                else:
                    SE.copy(dstb[:, kbase * 128:(kbase + take) * 128],
                            bank[:, :take * 128])
            ln1_subtile(d_xkv[r0:r0 + 128, :], pt_ps, [a_pt[0], a_pt[0]],
                        [a_pe[0], a_pe[0]], ln_h, ln_l, emitk)
            spl_v.wait(TE)
            spl_s.wait(TE)
            for g in range(KVH):
                bi = g % 2
                pa_free[bi].wait(TE)
                for k in range(KCH):
                    wcol = slice(k * 512 + g * 128, k * 512 + (g + 1) * 128)
                    lncol = slice(k * 128, (k + 1) * 128)
                    TE.matmul(pa_ps[bi][:, :128], wk_h[:, wcol], lnf_h[:, lncol],
                              start=(k == 0), stop=False)
                    TE.matmul(pa_ps[bi][:, :128], wk_h[:, wcol], lnf_l[:, lncol],
                              start=False, stop=False)
                    last = TE.matmul(pa_ps[bi][:, :128], wk_l[:, wcol], lnf_h[:, lncol],
                                     start=False, stop=(k == KCH - 1))
                pa_done[bi].inc(last)
                kcols = slice(g * 512 + tt * 128, g * 512 + (tt + 1) * 128)
                pa_done[bi].wait(SE)
                spl_s.inc(SE.copy(k_h[:, kcols], pa_ps[bi][:, :128]))
                spl_s.wait(VE)
                spl_v.inc(VE.tensor_tensor(k_l[:, kcols], pa_ps[bi][:, :128],
                                           k_h[:, kcols], op=OP.subtract))
                pa_free[bi].inc(VE.engine_nop())
            bi = tt % 2
            pa_free[bi].wait(TE)
            for k in range(KCH):
                lncol = slice(k * 128, (k + 1) * 128)
                wcols = slice(k * 512, (k + 1) * 512)
                TE.matmul(pa_ps[bi][:, :], lnf_h[:, lncol], wv_h[:, wcols],
                          start=(k == 0), stop=False)
                TE.matmul(pa_ps[bi][:, :], lnf_h[:, lncol], wv_l[:, wcols],
                          start=False, stop=False)
                last = TE.matmul(pa_ps[bi][:, :], lnf_l[:, lncol], wv_h[:, wcols],
                                 start=False, stop=(k == KCH - 1))
            pa_done[bi].inc(last)
            vcols = slice(tt * 512, (tt + 1) * 512)
            pa_done[bi].wait(SE)
            spl_s.inc(SE.copy(v_h[:, vcols], pa_ps[bi][:, :]))
            spl_s.wait(VE)
            spl_v.inc(VE.tensor_tensor(v_l[:, vcols], pa_ps[bi][:, :], v_h[:, vcols],
                                       op=OP.subtract))
            pa_free[bi].inc(VE.engine_nop())
        return spl_v.n, spl_s.n

    ALVL = int(os.environ.get("KB_ALVL", "9"))

    AMARKS = {"sc": {}, "cm": {}, "vb": {}, "sp": {}, "ph": {}, "pl": {},
              "tr": {}, "ev": {}, "av": {}, "o": {}, "sd": {}}

    def attend_step(phase, step, kv_bufs, aps, o_acc, kv_v, kv_s, q_v, q_s):
        (k_h, k_l, v_h, v_l) = kv_bufs
        (sc_ps, pt_ps, pa_ps) = aps
        NB = len(sc_ps)
        NC = len(pt_ps)
        par_in = step % 2
        par_out = (step + 1) % 2
        gcol = (0 if phase == 0 else ASTEPS) + step
        gate_col = gates_s[:, gcol:gcol + 1]
        n = 64
        u0 = unit_ctr[0]
        unit_ctr[0] += n
        M = AMARKS
        ln_ev_mark = (a_pe[0], a_pe[0].n)
        first = [True]

        def wai(eng, mark):
            if mark is not None:
                eng.wait_ge(mark[0].h, mark[1])

        def st_scores(i):
            u = u0 + i
            b = u % NB
            q, h = divmod(i, H)
            g = h // NREP
            wai(TE, M["o"].get(u - NB))
            if first[0]:
                TE.wait_ge(spl_v.h, max(kv_v, q_v))
                TE.wait_ge(spl_s.h, max(kv_s, q_s))
                first[0] = False
            qh = q_h[:, h * 512 + q * 128: h * 512 + (q + 1) * 128]
            ql = q_l[:, h * 512 + q * 128: h * 512 + (q + 1) * 128]
            TE.matmul(sc_ps[b][:, :], qh, k_h[:, g * 512:(g + 1) * 512],
                      start=True, stop=False)
            TE.matmul(sc_ps[b][:, :], qh, k_l[:, g * 512:(g + 1) * 512],
                      start=False, stop=False)
            last = TE.matmul(sc_ps[b][:, :], ql, k_h[:, g * 512:(g + 1) * 512],
                             start=False, stop=True)
            a_sc[0].inc(last)
            M["sc"][u] = (a_sc[0], a_sc[0].n)

        def st_stats(i):
            u = u0 + i
            b = u % NB
            q, h = divmod(i, H)
            uc = i
            ucs = slice(uc, uc + 1)
            wai(VE, M["sc"][u])
            wai(VE, M["vb"].get(u - 64))
            if step == 0:
                VE.tensor_tensor(sc_ps[b][:, :], sc_ps[b][:, :],
                                 masks_s[:, q * CH:(q + 1) * CH], op=OP.add)
                VE.drain()
            VE.reduce_max(cmx[:, ucs], sc_ps[b][:, :], axis=AX)
            VE.drain()
            mi = m_run[:, par_in * 64 + uc: par_in * 64 + uc + 1]
            mo = m_run[:, par_out * 64 + uc: par_out * 64 + uc + 1]
            VE.scalar_tensor_tensor(mo, cmx[:, ucs], gate_col, mi,
                                    op0=OP.add, op1=OP.max)
            VE.drain()
            VE.scalar_tensor_tensor(bi_p[:, ucs], mo, -SCALE, gate_col,
                                    op0=OP.mult, op1=OP.add)
            a_vb.inc(VE.tensor_scalar(bi_a[:, ucs], mo, -SCALE, None, op0=OP.mult))
            M["vb"][u] = (a_vb, a_vb.n)

        def st_exp(i):
            u = u0 + i
            b = u % NB
            ps = u % 3
            uc = i
            ucs = slice(uc, uc + 1)
            mi = m_run[:, par_in * 64 + uc: par_in * 64 + uc + 1]
            wai(SE, M["vb"][u])
            wai(SE, M["tr"].get(u - 3))
            SE.activation(alph[:, ucs], mi, AF.Exp, bias=bi_a[:, ucs], scale=SCALE)
            a_ph.inc(SE.activation(p_h[:, ps * CH:(ps + 1) * CH], sc_ps[b][:, :],
                                   AF.Exp, bias=bi_p[:, ucs], scale=SCALE))
            M["ph"][u] = (a_ph, a_ph.n)
            a_sp.inc(SE.activation(p_f[:, ps * CH:(ps + 1) * CH], sc_ps[b][:, :],
                                   AF.Exp, bias=bi_p[:, ucs], scale=SCALE,
                                   accum_out=csum[:, ucs]))
            M["sp"][u] = (a_sp, a_sp.n)

        def st_plo(i):
            u = u0 + i
            ps = u % 3
            uc = i
            ucs = slice(uc, uc + 1)
            wai(GP, M["ph"][u])
            wai(GP, M["sp"][u])
            a_pl.inc(GP.tensor_tensor(p_l[:, ps * CH:(ps + 1) * CH],
                                      p_f[:, ps * CH:(ps + 1) * CH],
                                      p_h[:, ps * CH:(ps + 1) * CH],
                                      op=OP.subtract))
            M["pl"][u] = (a_pl, a_pl.n)
            wai(VE, M["sp"][u])
            wai(VE, M["sd"].get(u - 64))
            vc2.inc(VE.scalar_tensor_tensor(
                sden[:, par_out * 64 + uc: par_out * 64 + uc + 1],
                sden[:, par_in * 64 + uc: par_in * 64 + uc + 1],
                alph[:, ucs], csum[:, ucs], op0=OP.mult, op1=OP.add))
            M["sd"][u] = (vc2, vc2.n)

        def st_trans(i):
            u = u0 + i
            c = u % NC
            ps = u % 3
            wai(TE, M["ev"].get(u - NC))
            wai(TE, ln_ev_mark)
            wai(TE, M["ph"][u])
            wai(TE, M["pl"][u])
            for k in range(4):
                TE.transpose(pt_ps[c][:, k * 128:(k + 1) * 128],
                             p_h[:, ps * CH + k * 128: ps * CH + (k + 1) * 128],
                             ident_s[:, :])
            last = None
            for k in range(4):
                last = TE.transpose(pt_ps[c][:, 512 + k * 128: 512 + (k + 1) * 128],
                                    p_l[:, ps * CH + k * 128: ps * CH + (k + 1) * 128],
                                    ident_s[:, :])
            a_pt[0].inc(last)
            M["tr"][u] = (a_pt[0], a_pt[0].n)

        def st_evac(i):
            u = u0 + i
            c = u % NC
            ps = u % 3
            eng = SE if c == 1 else VE
            wai(eng, M["tr"][u])
            wai(eng, M["av"].get(u - 3))
            if eng is SE:
                a_pe[0].inc(SE.copy(pt_hl[:, ps * 1024:(ps + 1) * 1024],
                                    pt_ps[c][:, :]))
            else:
                a_pe[0].inc(VE.tensor_copy(pt_hl[:, ps * 1024:(ps + 1) * 1024],
                                           pt_ps[c][:, :]))
            M["ev"][u] = (a_pe[0], a_pe[0].n)

        def st_av(i):
            u = u0 + i
            b = u % NB
            ps = u % 3
            q, h = divmod(i, H)
            g = h // NREP
            wai(TE, M["ev"][u])
            wai(TE, M["sp"][u])
            last = None
            for k in range(4):
                pth = pt_hl[:, ps * 1024 + k * 128: ps * 1024 + (k + 1) * 128]
                ptl = pt_hl[:, ps * 1024 + 512 + k * 128: ps * 1024 + 512 + (k + 1) * 128]
                vh = v_h[:, k * 512 + g * 128: k * 512 + (g + 1) * 128]
                vl = v_l[:, k * 512 + g * 128: k * 512 + (g + 1) * 128]
                TE.matmul(sc_ps[b][:, :128], pth, vh, start=(k == 0), stop=False)
                TE.matmul(sc_ps[b][:, :128], pth, vl, start=False, stop=False)
                last = TE.matmul(sc_ps[b][:, :128], ptl, vh, start=False,
                                 stop=(k == 3))
            t2v.inc(last)
            M["av"][u] = (t2v, t2v.n)

        def st_oupd(i):
            u = u0 + i
            b = u % NB
            q, h = divmod(i, H)
            uc = i
            ucs = slice(uc, uc + 1)
            wai(VE, M["av"][u])
            osl = o_acc[:, q * DM + h * 128: q * DM + (h + 1) * 128]
            vc2.inc(VE.scalar_tensor_tensor(osl, osl, alph[:, ucs],
                                            sc_ps[b][:, :128],
                                            op0=OP.mult, op1=OP.add))
            M["o"][u] = (vc2, vc2.n)

        for cyc in range(n + 7):
            for off, fn in ((6, st_oupd), (5, st_evac), (5, st_av), (4, st_plo),
                            (4, st_trans), (2, st_exp), (1, st_stats),
                            (0, st_scores)):
                j = cyc - off
                if 0 <= j < n:
                    fn(j)

    def wo_phase(phase, nsteps, o_acc):
        par_fin = nsteps % 2
        vc2.wait(VE)
        VE.reciprocal(rden[:, :64], sden[:, par_fin * 64: par_fin * 64 + 64])
        vc2.inc(VE.drain())
        otr_ctx = ExitStack()
        otr = [otr_ctx.enter_context(
            nc.psum_tensor(f"otr{phase}_{i}", [128, 1024], BF16)) for i in range(2)]
        tr_marks = []
        for q in range(4):
            for h in range(H):
                i = q * H + h
                uc = i
                osl = o_acc[:, q * DM + h * 128: q * DM + (h + 1) * 128]
                pslot = i % 4
                pcol = slice(pslot * 128, pslot * 128 + 128)
                if i >= 4:
                    VE.wait_ge(t2v.h, tr_marks[i - 4])
                VE.tensor_scalar(p_h[:, pcol], osl, rden[:, uc:uc + 1], None,
                                 op0=OP.mult)
                VE.drain()
                VE.scalar_tensor_tensor(p_l[:, pcol], osl, rden[:, uc:uc + 1],
                                        p_h[:, pcol], op0=OP.mult, op1=OP.subtract)
                v2t.inc(VE.drain())
                v2t.wait(TE)
                bi = i % 2
                tr_free[bi].wait(TE)
                TE.transpose(otr[bi][:, :128], p_h[:, pcol], ident_s[:, :])
                last = TE.transpose(otr[bi][:, 128:256], p_l[:, pcol], ident_s[:, :])
                tr_done[bi].inc(last)
                t2v.inc(TE.drain())
                tr_marks.append(t2v.n)
                eng = VE if bi == 0 else SE
                tr_done[bi].wait(eng)
                dcol = slice(h * 512 + q * 128, h * 512 + (q + 1) * 128)
                if eng is VE:
                    VE.tensor_copy(q_h[:, dcol], otr[bi][:, :128])
                    VE.tensor_copy(q_l[:, dcol], otr[bi][:, 128:256])
                    vc2.inc(VE.drain())
                    tr_free[bi].inc(VE.engine_nop())
                else:
                    SE.copy(q_h[:, dcol], otr[bi][:, :128])
                    SE.copy(q_l[:, dcol], otr[bi][:, 128:256])
                    sc2.inc(SE.drain())
                    tr_free[bi].inc(SE.drain())
        vc2.wait(TE)
        sc2.wait(TE)
        bar()
        otr_ctx.close()
        wctx = ExitStack()
        wqf = wctx.enter_context(nc.sbuf_tensor(f"wof{phase}", [128, 2 * 1024], F32))
        wqh = wctx.enter_context(nc.sbuf_tensor(f"woh{phase}", [128, 2 * 1024], BF16))
        wql = wctx.enter_context(nc.sbuf_tensor(f"wol{phase}", [128, 2 * 1024], BF16))
        wops_ctx = ExitStack()
        wops = [wops_ctx.enter_context(
            nc.psum_tensor(f"wops{phase}_{i}", [128, 512], F32)) for i in range(8)]
        wo_marks = []
        li = 0
        for ocg in range(2):
            for k in range(KCH):
                slot = li % 2
                wsl = wqf[:, slot * 1024:(slot + 1) * 1024]
                if li >= 2:
                    SY.wait_ge(t2v.h, wo_marks[li - 2])
                s_wq[slot].inc(SY.dma_start(
                    out=wsl,
                    in_=d_wo[k * 128:(k + 1) * 128, ocg * 1024:(ocg + 1) * 1024]), 16)
                s_wq[slot].wait(SE)
                hcols = slice(slot * 1024, (slot + 1) * 1024)
                SE.copy(wqh[:, hcols], wsl)
                sc2.inc(SE.drain())
                sc2.wait(VE)
                VE.tensor_tensor(wql[:, hcols], wsl, wqh[:, hcols], op=OP.subtract)
                vc2.inc(VE.drain())
                vc2.wait(TE)
                sc2.wait(TE)
                for tt in range(4):
                    for oc in range(2):
                        pb = wops[tt * 2 + oc]
                        wh = wqh[:, slot * 1024 + oc * 512: slot * 1024 + (oc + 1) * 512]
                        wl = wql[:, slot * 1024 + oc * 512: slot * 1024 + (oc + 1) * 512]
                        lh = q_h[:, k * 512 + tt * 128: k * 512 + (tt + 1) * 128]
                        ll = q_l[:, k * 512 + tt * 128: k * 512 + (tt + 1) * 128]
                        TE.matmul(pb[:, :], lh, wh, start=(k == 0), stop=False)
                        TE.matmul(pb[:, :], ll, wh, start=False, stop=False)
                        last = TE.matmul(pb[:, :], lh, wl, start=False, stop=(k == KCH - 1))
                t2v.inc(last)
                wo_marks.append(t2v.n)
                li += 1
            t2v.wait(VE)
            for tt in range(4):
                slot = tile_ctr[0] % 2
                tile_ctr[0] += 1
                sl = x_sb[:, slot * DM:(slot + 1) * DM]
                xf = s_xf[slot]
                xf.wait(SY)
                r0 = phase * QB + tt * 128
                s_x[slot].inc(SY.dma_start(out=sl[:, :1024],
                                           in_=d_xq[r0:r0 + 128,
                                                    ocg * 1024:(ocg + 1) * 1024]), 16)
                s_x[slot].wait(VE)
                for oc in range(2):
                    VE.tensor_tensor(sl[:, 1024 + oc * 512: 1024 + (oc + 1) * 512],
                                     wops[tt * 2 + oc][:, :],
                                     sl[:, oc * 512:(oc + 1) * 512], op=OP.add)
                vc2.inc(VE.drain())
                v2t.inc(VE.engine_nop())
                vc2.wait(SY)
                st_x[slot].inc(SY.dma_start(
                    out=d_rsp[r0:r0 + 128, ocg * 1024:(ocg + 1) * 1024],
                    in_=sl[:, 1024:2048]), 16)
                st_x[slot].wait(SY)
                xf.inc(SY.nop(), 1)
            v2t.wait(TE)
        bar()
        wops_ctx.close()
        wctx.close()

    for phase, nsteps in ((0, ASTEPS), (1, BSTEPS)):
        VE.memset(m_run[:, :64], NEG)
        VE.memset(sden[:, :64], 0.0)
        vc2.inc(VE.drain())
        q_v, q_s = q_build2(phase)
        if stage_done():
            return nc
        o_ctx = ExitStack()
        o_acc = o_ctx.enter_context(nc.sbuf_tensor(f"o_acc{phase}", [128, 4 * DM], F32))
        for qq in range(4):
            VE.memset(o_acc[:, qq * DM:(qq + 1) * DM], 0.0)
        vc2.inc(VE.drain())
        kv_ctx = ExitStack()
        k_h = kv_ctx.enter_context(nc.sbuf_tensor(f"k_h{phase}", [128, KVH * 512], BF16))
        k_l = kv_ctx.enter_context(nc.sbuf_tensor(f"k_l{phase}", [128, KVH * 512], BF16))
        v_h = kv_ctx.enter_context(nc.sbuf_tensor(f"v_h{phase}", [128, 4 * 512], BF16))
        v_l = kv_ctx.enter_context(nc.sbuf_tensor(f"v_l{phase}", [128, 4 * 512], BF16))
        aps_ctx = ExitStack()
        sc_ps = [aps_ctx.enter_context(nc.psum_tensor(f"sc{phase}_{i}", [128, 512], F32))
                 for i in range(6)]
        pt_ps = [aps_ctx.enter_context(nc.psum_tensor(f"pt{phase}_{i}", [128, 1024], BF16))
                 for i in range(2)]
        pa_ps = [sc_ps[0], sc_ps[1]]
        kv_bufs = (k_h, k_l, v_h, v_l)
        aps = (sc_ps, pt_ps, pa_ps)
        for step in range(nsteps):
            kv_v, kv_s = kv_chunk(phase, step, kv_bufs, aps)
            if not os.environ.get("KB_NO_ATTEND"):
                attend_step(phase, step, kv_bufs, aps, o_acc, kv_v, kv_s, q_v, q_s)
            if not os.environ.get("KB_NO_SCALE"):
                emit_scale_tiles(32)
        if stage_done():
            return nc
        bar()
        aps_ctx.close()
        kv_ctx.close()
        wo_phase(phase, nsteps, o_acc)
        if stage_done():
            return nc
        bar()
        o_ctx.close()
    emit_scale_tiles(384)

    sc2.wait(VE)
    for wi in range(3):
        VE.reduce_sum(asum[:, wi:wi + 1], wacc[:, wi * 128:(wi + 1) * 128], axis=AX)
    VE.drain()
    VE.tensor_copy(wabs_t[:, 0:3], asum[:, 0:3])
    VE.drain()
    VE.tensor_tensor(asum[:, 4:7], asum[:, 0:3], wabs_t[:, 0:3], op=OP.subtract)
    VE.drain()
    VE.tensor_copy(wabs_t[:, 4:7], asum[:, 4:7])
    VE.drain()
    VE.memset(wabs_t[:, 3:4], 0.0)
    VE.memset(wabs_t[:, 7:8], 0.0)
    vc2.inc(VE.drain())
    v2t.inc(VE.engine_nop())
    v2t.wait(TE)
    gp_done.wait(TE)
    scp_ctx = ExitStack()
    scp = scp_ctx.enter_context(nc.psum_tensor("scp", [128, 8], F32))
    last = TE.matmul(scp[:, :], ones_b[:, :], wabs_t[:, :8], start=True, stop=True)
    t2v.inc(last)
    t2v.wait(VE)
    VE.tensor_copy(wsml[:, 8:16], scp[:, 0:8])
    VE.drain()
    VE.tensor_tensor(wsml[:, 0:3], wsml[:, 8:11], wsml[:, 12:15], op=OP.add)
    VE.drain()
    VE.tensor_scalar(wsml[:, 0:3], wsml[:, 0:3], float(2.0 ** -24), EPS,
                     op0=OP.mult, op1=OP.add)
    VE.drain()
    VE.tensor_scalar(wsml[:, 4:7], wsml[:, 0:3], 0.5, None, op0=OP.mult)
    vc2.inc(VE.drain())
    v2t.inc(VE.engine_nop())
    v2t.wait(TE)
    bar()
    scp_ctx.close()
    attn_ctx.close()
    if stage_done():
        return nc

    WSG, WSU, WSD = 0, 1, 2
    ffn_ctx = ExitStack()
    fst = ffn_ctx.enter_context(nc.sbuf_tensor("fst", [128, 48], F32))
    fscr_ctx = ExitStack()
    f32a = fscr_ctx.enter_context(nc.sbuf_tensor("f32a", [128, 2 * DM], F32))
    f32b = fscr_ctx.enter_context(nc.sbuf_tensor("f32b", [128, 2 * DM], F32))
    bf_a = fscr_ctx.enter_context(nc.sbuf_tensor("bf_a", [128, 2 * DM], BF16))
    i32a = fscr_ctx.enter_context(nc.sbuf_tensor("i32a", [128, DM], I32))

    def quant_act_tile(src_f32, rr_ap, dst_bf, W):
        VE.tensor_scalar(i32a[:, :W], src_f32, rr_ap, None, op0=OP.mult)
        VE.drain()
        VE.tensor_copy(f32b[:, :W], i32a[:, :W])
        VE.drain()
        VE.scalar_tensor_tensor(f32b[:, DM:DM + W], src_f32, rr_ap, f32b[:, :W],
                                op0=OP.mult, op1=OP.subtract)
        VE.drain()
        VE.tensor_scalar(bf_a[:, :W], f32b[:, DM:DM + W], 0.5, None, op0=OP.is_ge)
        VE.tensor_scalar(bf_a[:, DM:DM + W], f32b[:, DM:DM + W], -1.0, 0.5,
                         op0=OP.mult, op1=OP.is_ge)
        VE.drain()
        VE.tensor_tensor(f32b[:, DM:DM + W], bf_a[:, :W], bf_a[:, DM:DM + W],
                         op=OP.subtract)
        VE.drain()
        VE.tensor_tensor(dst_bf, f32b[:, :W], f32b[:, DM:DM + W], op=OP.add)
        vc2.inc(VE.drain())

    xq_ctx = ExitStack()
    xqfm = xq_ctx.enter_context(nc.sbuf_tensor("xqfm", [128, KCH * NTOK], BF16))
    f1_ps = ExitStack()
    f1tr = [f1_ps.enter_context(nc.psum_tensor("f1tr0", [128, 1024], BF16)),
            f1_ps.enter_context(nc.psum_tensor("f1tr1", [128, 1024], BF16))]
    f1_marks = []
    f1_tr_marks = []
    for tt in range(8):
        r0 = tt * 128
        slot = tt % 2
        sl = f32a[:, slot * DM:(slot + 1) * DM]
        if tt >= 2:
            SY.wait_ge(vc2.h, f1_marks[tt - 2])
        s_fa[slot].inc(SY.dma_start(out=sl, in_=d_rsp[r0:r0 + 128, :]), 16)
        s_fa[slot].wait(VE)
        for c in range(4):
            VE.bn_stats(bn6[:, 6 * c:6 * (c + 1)], sl[:, 512 * c:512 * (c + 1)])
        VE.drain()
        VE.bn_aggr(bn2[:, :], bn6[:, :])
        VE.drain()
        VE.tensor_scalar(lnsc[:, 0:1], bn2[:, 0:1], -1.0, None, op0=OP.mult)
        vc2.inc(VE.drain())
        vc2.wait(SE)
        SE.activation(lnsc[:, 1:2], bn2[:, 1:2], AF.Sqrt, bias=lnsc[:, 7:8], scale=1.0)
        sc2.inc(SE.drain())
        sc2.wait(VE)
        VE.reciprocal(lnsc[:, 2:3], lnsc[:, 1:2])
        VE.drain()
        VE.tensor_scalar(sl, sl, lnsc[:, 0:1], None, op0=OP.add)
        VE.drain()
        VE.reduce_max(lnsc[:, 4:5], sl, axis=AX, apply_absolute_value=True)
        VE.drain()
        VE.scalar_tensor_tensor(fst[:, tt:tt + 1], lnsc[:, 4:5],
                                float(1.0 / 127.0), lnsc[:, 2:3],
                                op0=OP.mult, op1=OP.mult)
        VE.drain()
        VE.tensor_scalar(fst[:, tt:tt + 1], fst[:, tt:tt + 1], EPS, None,
                         op0=OP.add)
        VE.drain()
        VE.reciprocal(lnsc[:, 4:5], fst[:, tt:tt + 1])
        VE.drain()
        VE.tensor_tensor(lnsc[:, 4:5], lnsc[:, 4:5], lnsc[:, 2:3], op=OP.mult)
        VE.drain()
        if tt >= 1:
            VE.wait_ge(t2v.h, f1_tr_marks[tt - 1])
        quant_act_tile(sl, lnsc[:, 4:5], bf_a[:, :DM], DM)
        f1_marks.append(vc2.n)
        vc2.wait(TE)
        done = 0
        while done < KCH:
            bi = (done // 8) % 2
            take = min(8, KCH - done)
            tr_free[bi].wait(TE)
            last = None
            for i in range(take):
                last = TE.transpose(f1tr[bi][:, i * 128:(i + 1) * 128],
                                    bf_a[:, (done + i) * 128:(done + i + 1) * 128],
                                    ident_s[:, :])
            tr_done[bi].inc(last)
            eng = VE if bi == 0 else SE
            tr_done[bi].wait(eng)
            for i in range(take):
                col = (done + i) * NTOK + tt * 128
                if eng is VE:
                    VE.tensor_copy(xqfm[:, col:col + 128],
                                   f1tr[bi][:, i * 128:(i + 1) * 128])
                else:
                    SE.copy(xqfm[:, col:col + 128],
                            f1tr[bi][:, i * 128:(i + 1) * 128])
            if eng is VE:
                vc2.inc(VE.drain())
                tr_free[bi].inc(VE.engine_nop())
            else:
                sc2.inc(SE.drain())
                tr_free[bi].inc(SE.drain())
            done += take
        t2v.inc(TE.drain())
        f1_tr_marks.append(t2v.n)
    vc2.wait(VE)
    VE.tensor_scalar(fst[:, 16:24], fst[:, 0:8], wsml[:, 0:1], None, op0=OP.mult)
    VE.tensor_scalar(fst[:, 24:32], fst[:, 0:8], wsml[:, 1:2], None, op0=OP.mult)
    VE.memset(fst[:, 32:40], 0.0)
    vc2.inc(VE.drain())
    if stage_done():
        return nc
    bar()
    f1_ps.close()

    f2_ctx = ExitStack()
    wgq = f2_ctx.enter_context(nc.sbuf_tensor("wgq", [128, KCH * 512], BF16))
    wuq = f2_ctx.enter_context(nc.sbuf_tensor("wuq", [128, KCH * 512], BF16))
    g_sb = f2_ctx.enter_context(nc.sbuf_tensor("g_sb", [128, 8 * 512], F32))
    h_sb = f2_ctx.enter_context(nc.sbuf_tensor("h_sb", [128, 2 * 512], F32))
    f2_ps = ExitStack()
    fps = [f2_ps.enter_context(nc.psum_tensor(f"fps{i}", [128, 512], F32))
           for i in range(8)]
    fps_free = [S(nc, f"fpsf{i}") for i in range(8)]
    fps_done = [0] * 8
    fq_marks = []
    fq_li = [0]
    h_st_marks = []

    def quant_w_tiles(dsrc, dst, wi, oc, mm_guard):
        for k in range(KCH):
            li = fq_li[0]
            fq_li[0] += 1
            slot = li % 2
            sl = f32a[:, slot * DM: slot * DM + 512]
            if li >= 2:
                SY.wait_ge(vc2.h, fq_marks[li - 2])
            s_fa[slot].inc(SY.dma_start(
                out=sl, in_=dsrc[k * 128:(k + 1) * 128,
                                 oc * 512:(oc + 1) * 512]), 16)
            s_fa[slot].wait(VE)
            if k == 0:
                VE.wait_ge(t2v.h, mm_guard)
            bslot = li % 2
            bc2 = slice(bslot * 1024 + 512, bslot * 1024 + 1024)
            VE.tensor_scalar(bf_a[:, bc2], sl, -1.0, wsml[:, 4 + wi:5 + wi],
                             op0=OP.mult, op1=OP.is_ge)
            VE.drain()
            vc2.inc(VE.scalar_tensor_tensor(dst[:, k * 512:(k + 1) * 512], sl,
                                            wsml[:, 4 + wi:5 + wi], bf_a[:, bc2],
                                            op0=OP.is_ge, op1=OP.subtract))
            fq_marks.append(vc2.n)

    mm_guard = [t2v.n]
    for oc in range(FI // 512):
        quant_w_tiles(d_wgT, wgq, WSG, oc, mm_guard[0])
        quant_w_tiles(d_wuT, wuq, WSU, oc, mm_guard[0])
        vc2.wait(TE)
        for tt in range(8):
            pb = fps[tt % 4]
            fps_free[tt % 4].wait(TE)
            for k in range(KCH):
                last = TE.matmul(pb[:, :],
                                 xqfm[:, k * NTOK + tt * 128: k * NTOK + (tt + 1) * 128],
                                 wgq[:, k * 512:(k + 1) * 512],
                                 start=(k == 0), stop=(k == KCH - 1))
            t2v.inc(last)
            fps_done[tt % 4] = t2v.n
            SE.wait_ge(t2v.h, fps_done[tt % 4])
            SE.activation(g_sb[:, tt * 512:(tt + 1) * 512], pb[:, :], AF.Silu,
                          scale=fst[:, 16 + tt:17 + tt])
            sc2.inc(SE.drain())
            fps_free[tt % 4].inc(SE.drain())
        for tt in range(8):
            pb = fps[4 + tt % 4]
            fps_free[4 + tt % 4].wait(TE)
            for k in range(KCH):
                last = TE.matmul(pb[:, :],
                                 xqfm[:, k * NTOK + tt * 128: k * NTOK + (tt + 1) * 128],
                                 wuq[:, k * 512:(k + 1) * 512],
                                 start=(k == 0), stop=(k == KCH - 1))
            t2v.inc(last)
            fps_done[4 + tt % 4] = t2v.n
            hidx = oc * 8 + tt
            hs = hidx % 2
            hsl = h_sb[:, hs * 512:(hs + 1) * 512]
            VE.wait_ge(t2v.h, fps_done[4 + tt % 4])
            sc2.wait(VE)
            if hidx >= 2:
                st_h[hs].wait(VE)
            VE.scalar_tensor_tensor(hsl, pb[:, :], fst[:, 24 + tt:25 + tt],
                                    g_sb[:, tt * 512:(tt + 1) * 512],
                                    op0=OP.mult, op1=OP.mult)
            VE.drain()
            fps_free[4 + tt % 4].inc(VE.engine_nop())
            VE.reduce_max(lnsc[:, 5:6], hsl, axis=AX, apply_absolute_value=True)
            VE.drain()
            VE.tensor_tensor(fst[:, 32 + tt:33 + tt], fst[:, 32 + tt:33 + tt],
                             lnsc[:, 5:6], op=OP.max)
            vc2.inc(VE.drain())
            vc2.wait(SY)
            st_h[hs].inc(SY.dma_start(
                out=d_hsp[tt * 128:(tt + 1) * 128, oc * 512:(oc + 1) * 512],
                in_=hsl), 16)
        mm_guard[0] = t2v.n
    vc2.wait(VE)
    VE.tensor_scalar(fst[:, 32:40], fst[:, 32:40], float(1.0 / 127.0), EPS,
                     op0=OP.mult, op1=OP.add)
    VE.drain()
    VE.reciprocal(fst[:, 40:48], fst[:, 32:40])
    VE.drain()
    VE.tensor_scalar(fst[:, 32:40], fst[:, 32:40], wsml[:, 2:3], None, op0=OP.mult)
    vc2.inc(VE.drain())
    st_h[0].wait(SY)
    st_h[1].wait(SY)
    if stage_done():
        return nc
    bar()
    f2_ps.close()
    f2_ctx.close()
    xq_ctx.close()

    f3_ctx = ExitStack()
    hqfm = f3_ctx.enter_context(nc.sbuf_tensor("hqfm", [128, 64 * NTOK], BF16))
    f3_ps = ExitStack()
    f3tr = [f3_ps.enter_context(nc.psum_tensor("f3tr0", [128, 1024], BF16)),
            f3_ps.enter_context(nc.psum_tensor("f3tr1", [128, 1024], BF16))]
    f3_marks = []
    f3_tr_marks = []
    for tt in range(8):
        for pc in range(4):
            li = tt * 4 + pc
            slot = li % 2
            sl = f32a[:, slot * DM:(slot + 1) * DM]
            if li >= 2:
                SY.wait_ge(vc2.h, f3_marks[li - 2])
            s_fa[slot].inc(SY.dma_start(
                out=sl, in_=d_hsp[tt * 128:(tt + 1) * 128,
                                  pc * DM:(pc + 1) * DM]), 16)
            s_fa[slot].wait(VE)
            if li >= 1:
                VE.wait_ge(t2v.h, f3_tr_marks[li - 1])
            quant_act_tile(sl, fst[:, 40 + tt:41 + tt], bf_a[:, :DM], DM)
            f3_marks.append(vc2.n)
            vc2.wait(TE)
            done = 0
            while done < KCH:
                bi = (done // 8) % 2
                take = min(8, KCH - done)
                tr_free[bi].wait(TE)
                last = None
                for i in range(take):
                    last = TE.transpose(f3tr[bi][:, i * 128:(i + 1) * 128],
                                        bf_a[:, (done + i) * 128:(done + i + 1) * 128],
                                        ident_s[:, :])
                tr_done[bi].inc(last)
                eng = VE if bi == 0 else SE
                tr_done[bi].wait(eng)
                for i in range(take):
                    ic = pc * KCH + done + i
                    col = ic * NTOK + tt * 128
                    if eng is VE:
                        VE.tensor_copy(hqfm[:, col:col + 128],
                                       f3tr[bi][:, i * 128:(i + 1) * 128])
                    else:
                        SE.copy(hqfm[:, col:col + 128],
                                f3tr[bi][:, i * 128:(i + 1) * 128])
                if eng is VE:
                    vc2.inc(VE.drain())
                    tr_free[bi].inc(VE.engine_nop())
                else:
                    sc2.inc(SE.drain())
                    tr_free[bi].inc(SE.drain())
                done += take
            t2v.inc(TE.drain())
            f3_tr_marks.append(t2v.n)
    if stage_done():
        return nc
    bar()
    f3_ps.close()

    f4s_ctx = ExitStack()
    bf16_wd = f4s_ctx.enter_context(nc.sbuf_tensor("bf16_wd", [128, 2 * 512], BF16))
    f4_ps = ExitStack()
    dps = [f4_ps.enter_context(nc.psum_tensor(f"dps{i}", [128, 512], F32))
           for i in range(8)]
    dps_free = [S(nc, f"dpsf{i}") for i in range(8)]
    f4_marks = []
    f4_marks_mm = []
    f4_li = [0]
    for ocd in range(4):
        for ic in range(64):
            li = f4_li[0]
            f4_li[0] += 1
            slot = li % 2
            sl = f32a[:, slot * DM: slot * DM + 512]
            if li >= 2:
                SY.wait_ge(vc2.h, f4_marks[li - 2])
            s_fa[slot].inc(SY.dma_start(
                out=sl, in_=d_wdT[ic * 128:(ic + 1) * 128,
                                  ocd * 512:(ocd + 1) * 512]), 16)
            s_fa[slot].wait(VE)
            bslot = li % 2
            bc2 = slice(bslot * 1024 + 512, bslot * 1024 + 1024)
            VE.tensor_scalar(bf_a[:, bc2], sl, -1.0, wsml[:, 6:7],
                             op0=OP.mult, op1=OP.is_ge)
            VE.drain()
            if li >= 2:
                VE.wait_ge(t2v.h, f4_marks_mm[li - 2])
            vc2.inc(VE.scalar_tensor_tensor(bf16_wd[:, bslot * 512:(bslot + 1) * 512],
                                            sl, wsml[:, 6:7], bf_a[:, bc2],
                                            op0=OP.is_ge, op1=OP.subtract))
            f4_marks.append(vc2.n)
            vc2.wait(TE)
            if ic == 0:
                for tt in range(8):
                    dps_free[tt].wait(TE)
            last = None
            for tt in range(8):
                last = TE.matmul(dps[tt][:, :],
                                 hqfm[:, ic * NTOK + tt * 128: ic * NTOK + (tt + 1) * 128],
                                 bf16_wd[:, bslot * 512:(bslot + 1) * 512],
                                 start=(ic == 0), stop=(ic == 63))
            t2v.inc(last)
            f4_marks_mm.append(t2v.n)
        t2v.wait(VE)
        for tt in range(8):
            slot = tile_ctr[0] % 2
            tile_ctr[0] += 1
            sl = x_sb[:, slot * DM:(slot + 1) * DM]
            xf = s_xf[slot]
            xf.wait(SY)
            r0 = tt * 128
            s_x[slot].inc(SY.dma_start(out=sl[:, :512],
                                       in_=d_rsp[r0:r0 + 128,
                                                 ocd * 512:(ocd + 1) * 512]), 16)
            s_x[slot].wait(VE)
            VE.scalar_tensor_tensor(sl[:, 512:1024], dps[tt][:, :],
                                    fst[:, 32 + tt:33 + tt], sl[:, :512],
                                    op0=OP.mult, op1=OP.add)
            vc2.inc(VE.drain())
            dps_free[tt].inc(VE.engine_nop())
            vc2.wait(SY)
            st_x[slot].inc(SY.dma_start(out=d_out[r0:r0 + 128,
                                                  ocd * 512:(ocd + 1) * 512],
                                        in_=sl[:, 512:1024]), 16)
            st_x[slot].wait(SY)
            xf.inc(SY.nop(), 1)
    bar()
    f4_ps.close()
    f4s_ctx.close()
    f3_ctx.close()
    fscr_ctx.close()
    ffn_ctx.close()

    for sem in (st_x[0], st_x[1], st_h[0], st_h[1], s_x[0], s_x[1],
                s_wab[0], s_wab[1], s_wq[0], s_wq[1], s_fa[0], s_fa[1], s_cst):
        sem.wait(SY)
    ctx.close()
    return nc


def make_in_maps(L0_attn_x, L0_attn_wq, L0_attn_wk, L0_attn_wv, L0_attn_wo,
                 L0_ffn_w_gate, L0_ffn_w_up, L0_ffn_w_down):
    import ml_dtypes
    x = np.ascontiguousarray(np.asarray(L0_attn_x, dtype=np.float32))
    wq = np.ascontiguousarray(np.asarray(L0_attn_wq, dtype=np.float32))
    wk = np.ascontiguousarray(np.asarray(L0_attn_wk, dtype=np.float32))
    wv = np.ascontiguousarray(np.asarray(L0_attn_wv, dtype=np.float32))
    wo = np.ascontiguousarray(np.asarray(L0_attn_wo, dtype=np.float32))
    wgT = np.ascontiguousarray(np.asarray(L0_ffn_w_gate, dtype=np.float32).T)
    wuT = np.ascontiguousarray(np.asarray(L0_ffn_w_up, dtype=np.float32).T)
    wdT = np.ascontiguousarray(np.asarray(L0_ffn_w_down, dtype=np.float32).T)

    ident = np.eye(128, dtype=ml_dtypes.bfloat16)
    masks = np.zeros((4 * 128, CH), dtype=np.float32)
    for t in range(4):
        for i in range(128):
            masks[t * 128 + i, t * 128 + i + 1:] = NEG
    masks = masks.astype(ml_dtypes.bfloat16)

    in_maps = []
    for c in range(8):
        b, j = c // 4, c % 4
        blkA, blkB = j, 7 - j
        xq = np.concatenate([x[b, blkA * QB:(blkA + 1) * QB],
                             x[b, blkB * QB:(blkB + 1) * QB]], axis=0)
        xkv = np.zeros((STEPS * CH, DM), dtype=np.float32)
        xkv[0:CH] = x[b, blkA * QB:(blkA + 1) * QB]
        for s in range(1, blkA + 1):
            xkv[s * CH:(s + 1) * CH] = x[b, (s - 1) * CH:s * CH]
        ob = ASTEPS * CH
        xkv[ob:ob + CH] = x[b, blkB * QB:(blkB + 1) * QB]
        for s in range(1, blkB + 1):
            xkv[ob + s * CH: ob + (s + 1) * CH] = x[b, (s - 1) * CH:s * CH]
        gates = np.zeros((128, STEPS), dtype=np.float32)
        for s in range(ASTEPS):
            if s > blkA:
                gates[:, s] = NEG
        for s in range(BSTEPS):
            if s > blkB:
                gates[:, ASTEPS + s] = NEG
        in_maps.append({
            "xq_rows": np.ascontiguousarray(xq),
            "x_kv": xkv,
            "wq": wq, "wk": wk, "wv": wv, "wo": wo,
            "wgT": wgT, "wuT": wuT, "wdT": wdT,
            "masks": masks, "gates": gates, "ident": ident,
        })

    return in_maps


def assemble_out(results):
    out = np.empty((B, T, DM), dtype=np.float32)
    for c in range(8):
        b, j = c // 4, c % 4
        blkA, blkB = j, 7 - j
        o = results[c]["out"]
        out[b, blkA * QB:(blkA + 1) * QB] = o[:QB]
        out[b, blkB * QB:(blkB + 1) * QB] = o[QB:]
    return out


def kernel(**inputs):
    in_maps = make_in_maps(**inputs)
    nc = build_nc()
    res = run_bass_kernel_spmd(nc, in_maps, core_ids=list(range(8)))
    return assemble_out(res.results)


# version 14 (final)
# speedup vs baseline: 1.0208x; 1.0138x over previous
import os
import numpy as np
from contextlib import ExitStack

import concourse.bass as bass
import concourse.mybir as mybir
from concourse.bass_utils import run_bass_kernel_spmd
from concourse.alu_op_type import AluOpType as OP

F32 = mybir.dt.float32
BF16 = mybir.dt.bfloat16
I32 = mybir.dt.int32
AF = mybir.ActivationFunctionType
AX = mybir.AxisListType.X

B, T, DM = 2, 4096, 2048
H, KVH, HD = 16, 4, 128
NREP = H // KVH
QB = 512
CH = 512
ASTEPS, BSTEPS = 4, 8
STEPS = ASTEPS + BSTEPS
KCH = DM // 128
SCALE = float(1.0 / np.sqrt(np.float32(HD)))
NEG = -1.0e30
EPS = 1e-5
FI = 8192
NTOK = 1024


class S:
    def __init__(self, nc, name):
        self.h = nc._ctx.enter_context(nc.semaphore(name))
        self.n = 0

    def inc(self, instr, k=1):
        instr.then_inc(self.h, k)
        self.n += k
        return instr

    def wait(self, eng, n=None):
        eng.wait_ge(self.h, self.n if n is None else n)


def build_nc():
    nc = bass.Bass("TRN2", target_bir_lowering=False, debug=False)
    ctx = ExitStack()
    nc._ctx = ctx

    d_xq = nc.dram_tensor("xq_rows", [NTOK, DM], F32, kind="ExternalInput").ap()
    d_xkv = nc.dram_tensor("x_kv", [STEPS * CH, DM], F32, kind="ExternalInput").ap()
    d_wq = nc.dram_tensor("wq", [DM, DM], F32, kind="ExternalInput").ap()
    d_wk = nc.dram_tensor("wk", [DM, KVH * HD], F32, kind="ExternalInput").ap()
    d_wv = nc.dram_tensor("wv", [DM, KVH * HD], F32, kind="ExternalInput").ap()
    d_wo = nc.dram_tensor("wo", [DM, DM], F32, kind="ExternalInput").ap()
    d_wgT = nc.dram_tensor("wgT", [DM, FI], F32, kind="ExternalInput").ap()
    d_wuT = nc.dram_tensor("wuT", [DM, FI], F32, kind="ExternalInput").ap()
    d_wdT = nc.dram_tensor("wdT", [FI, DM], F32, kind="ExternalInput").ap()
    d_masks = nc.dram_tensor("masks", [4 * 128, CH], BF16, kind="ExternalInput").ap()
    d_gates = nc.dram_tensor("gates", [128, STEPS], F32, kind="ExternalInput").ap()
    d_ident = nc.dram_tensor("ident", [128, 128], BF16, kind="ExternalInput").ap()
    d_out = nc.dram_tensor("out", [NTOK, DM], F32, kind="ExternalOutput").ap()
    d_hsp = nc.dram_tensor("h_spill", [NTOK, FI], F32, kind="Internal").ap()
    d_rsp = nc.dram_tensor("res_spill", [NTOK, DM], F32, kind="Internal").ap()

    TE, VE, SE, SY, GP = nc.tensor, nc.vector, nc.scalar, nc.sync, nc.gpsimd

    s_cst = S(nc, "s_cst")
    s_x = [S(nc, "s_x0"), S(nc, "s_x1")]
    s_wab = [S(nc, "s_wab0"), S(nc, "s_wab1")]
    s_wq = [S(nc, "s_wq0"), S(nc, "s_wq1")]
    s_fa = [S(nc, "s_fa0"), S(nc, "s_fa1")]
    st_h = [S(nc, "st_h0"), S(nc, "st_h1")]
    st_x = [S(nc, "st_x0"), S(nc, "st_x1")]
    s_xf = [S(nc, "s_xf0"), S(nc, "s_xf1")]
    t2v = S(nc, "t2v")
    v2t = S(nc, "v2t")
    sc2 = S(nc, "sc2")
    vc2 = S(nc, "vc2")
    gp_done = S(nc, "gp_done")
    a_sc = [S(nc, "a_sc0"), S(nc, "a_sc1"), S(nc, "a_sc2")]
    a_bf = [S(nc, "a_bf0"), S(nc, "a_bf1"), S(nc, "a_bf2")]
    a_vb = S(nc, "a_vb")
    a_sp = S(nc, "a_sp")
    a_ph = S(nc, "a_ph")
    a_pl = S(nc, "a_pl")
    a_pt = [S(nc, "a_pt0"), S(nc, "a_pt1"), S(nc, "a_pt2")]
    a_pe = [S(nc, "a_pe0"), S(nc, "a_pe1"), S(nc, "a_pe2")]
    spl_v = S(nc, "spl_v")
    spl_s = S(nc, "spl_s")
    tr_done = [S(nc, "trd0"), S(nc, "trd1")]
    tr_free = [S(nc, "trf0"), S(nc, "trf1")]
    pa_done = [S(nc, "pad0"), S(nc, "pad1")]
    pa_free = [S(nc, "paf0"), S(nc, "paf1")]

    x_sb = ctx.enter_context(nc.sbuf_tensor("x_sb", [128, 2 * DM], F32))
    ident_s = ctx.enter_context(nc.sbuf_tensor("ident_s", [128, 128], BF16))
    gates_s = ctx.enter_context(nc.sbuf_tensor("gates_s", [128, STEPS], F32))
    ones_b = ctx.enter_context(nc.sbuf_tensor("ones_b", [128, 128], BF16))
    bn6 = ctx.enter_context(nc.sbuf_tensor("bn6", [128, 24], F32))
    bn2 = ctx.enter_context(nc.sbuf_tensor("bn2", [128, 2], F32))
    lnsc = ctx.enter_context(nc.sbuf_tensor("lnsc", [128, 8], F32))
    wabs = ctx.enter_context(nc.sbuf_tensor("wabs", [128, 2 * 1024], F32))
    asum = ctx.enter_context(nc.sbuf_tensor("asum", [128, 8], F32))
    wsml = ctx.enter_context(nc.sbuf_tensor("wsml", [128, 16], F32))
    wabs_t = ctx.enter_context(nc.sbuf_tensor("wabs_t", [128, 8], BF16))
    wacc = ctx.enter_context(nc.sbuf_tensor("wacc", [128, 384], F32))

    tile_ctr = [0]
    unit_ctr = [0]
    pbar = S(nc, "pbar")

    def bar():
        for eng in (TE, VE, SE, GP, SY):
            pbar.inc(eng.drain())
        for eng in (TE, VE, SE, GP, SY):
            pbar.wait(eng)

    STAGE_LIMIT = int(os.environ.get("KB_STAGE", "99"))
    stage_ctr = [0]

    def stage_done():
        stage_ctr[0] += 1
        if stage_ctr[0] < STAGE_LIMIT:
            return False
        bar()
        fin = S(nc, "fin")
        fin.inc(SY.dma_start(out=d_out[0:128, :], in_=x_sb[:, 0:DM]), 16)
        fin.wait(SY)
        return True


    def ln1_subtile(src_rows, banks, done_s, free_s, ln_h, ln_l, emit_dst):
        slot = tile_ctr[0] % 2
        tile_ctr[0] += 1
        sl = x_sb[:, slot * DM:(slot + 1) * DM]
        xf = s_xf[slot]
        xf.wait(SY)
        s_x[slot].inc(SY.dma_start(out=sl, in_=src_rows), 16)
        s_x[slot].wait(VE)
        for c in range(4):
            VE.bn_stats(bn6[:, 6 * c:6 * (c + 1)], sl[:, 512 * c:512 * (c + 1)])
        VE.drain()
        VE.bn_aggr(bn2[:, :], bn6[:, :])
        VE.drain()
        vc2.inc(VE.tensor_scalar(lnsc[:, 0:1], bn2[:, 0:1], -1.0, None, op0=OP.mult))
        vc2.wait(SE)
        sc2.inc(SE.activation(lnsc[:, 1:2], bn2[:, 1:2], AF.Sqrt, bias=lnsc[:, 7:8], scale=1.0))
        sc2.wait(VE)
        VE.reciprocal(lnsc[:, 2:3], lnsc[:, 1:2])
        VE.drain()
        spl_v.inc(VE.tensor_scalar(sl, sl, lnsc[:, 0:1], lnsc[:, 2:3], op0=OP.add, op1=OP.mult))
        released = False
        nb = len(banks)
        for half in range(2):
            hsl = sl[:, half * 1024:(half + 1) * 1024]
            spl_v.wait(SE)
            spl_s.inc(SE.copy(ln_h[:, :], hsl))
            spl_s.wait(VE)
            spl_v.inc(VE.tensor_tensor(ln_l[:, :], hsl, ln_h[:, :], op=OP.subtract))
            if half == 1:
                xf.inc(VE.engine_nop(), 1)
                released = True
            for is_lo, srcb in ((0, ln_h), (1, ln_l)):
                bi = (2 * half + is_lo) % nb
                free_s[bi].wait(TE)
                if is_lo == 0:
                    spl_s.wait(TE)
                else:
                    spl_v.wait(TE)
                last = None
                for i in range(8):
                    last = TE.transpose(banks[bi][:, i * 128:(i + 1) * 128],
                                        srcb[:, i * 128:(i + 1) * 128], ident_s[:, :])
                done_s[bi].inc(last)
                eng = VE if (2 * half + is_lo) % 2 == 0 else SE
                done_s[bi].wait(eng)
                emit_dst(eng, banks[bi], half * 8, 8, is_lo)
                if eng is VE:
                    spl_v.inc(VE.engine_nop())
                    free_s[bi].inc(VE.engine_nop())
                else:
                    spl_s.inc(SE.drain())
                    free_s[bi].inc(SE.drain())
        assert released

    attn_ctx = ExitStack()
    wk_h = attn_ctx.enter_context(nc.sbuf_tensor("wk_h", [128, KCH * 512], BF16))
    wk_l = attn_ctx.enter_context(nc.sbuf_tensor("wk_l", [128, KCH * 512], BF16))
    wv_h = attn_ctx.enter_context(nc.sbuf_tensor("wv_h", [128, KCH * 512], BF16))
    wv_l = attn_ctx.enter_context(nc.sbuf_tensor("wv_l", [128, KCH * 512], BF16))
    q_h = attn_ctx.enter_context(nc.sbuf_tensor("q_h", [128, H * 512], BF16))
    q_l = attn_ctx.enter_context(nc.sbuf_tensor("q_l", [128, H * 512], BF16))
    ln_h = attn_ctx.enter_context(nc.sbuf_tensor("ln_h", [128, 1024], BF16))
    ln_l = attn_ctx.enter_context(nc.sbuf_tensor("ln_l", [128, 1024], BF16))
    lnf_h = attn_ctx.enter_context(nc.sbuf_tensor("lnf_h", [128, DM], BF16))
    lnf_l = attn_ctx.enter_context(nc.sbuf_tensor("lnf_l", [128, DM], BF16))
    masks_s = attn_ctx.enter_context(nc.sbuf_tensor("masks_s", [128, 4 * CH], BF16))
    p_f = attn_ctx.enter_context(nc.sbuf_tensor("p_f", [128, 3 * CH], F32))
    p_h = attn_ctx.enter_context(nc.sbuf_tensor("p_h", [128, 3 * CH], BF16))
    p_l = attn_ctx.enter_context(nc.sbuf_tensor("p_l", [128, 3 * CH], BF16))
    pt_hl = attn_ctx.enter_context(nc.sbuf_tensor("pt_hl", [128, 6 * CH], BF16))
    m_run = attn_ctx.enter_context(nc.sbuf_tensor("m_run", [128, 128], F32))
    sden = attn_ctx.enter_context(nc.sbuf_tensor("sden", [128, 128], F32))
    cmx = attn_ctx.enter_context(nc.sbuf_tensor("cmx", [128, 64], F32))
    alph = attn_ctx.enter_context(nc.sbuf_tensor("alph", [128, 64], F32))
    csum = attn_ctx.enter_context(nc.sbuf_tensor("csum", [128, 64], F32))
    bi_p = attn_ctx.enter_context(nc.sbuf_tensor("bi_p", [128, 64], F32))
    bi_a = attn_ctx.enter_context(nc.sbuf_tensor("bi_a", [128, 64], F32))
    rden = attn_ctx.enter_context(nc.sbuf_tensor("rden", [128, 64], F32))

    s_cst.inc(SY.dma_start(out=ident_s[:, :], in_=d_ident), 16)
    s_cst.inc(SY.dma_start(out=gates_s[:, :], in_=d_gates), 16)
    for t in range(4):
        s_cst.inc(SY.dma_start(out=masks_s[:, t * CH:(t + 1) * CH],
                               in_=d_masks[t * 128:(t + 1) * 128, :]), 16)
    s_cst.wait(TE)
    s_cst.wait(VE)
    s_cst.wait(SE)
    GP.memset(ones_b[:, :], 1.0)
    gp_done.inc(GP.engine_nop())
    VE.memset(asum[:, :], 0.0)
    VE.memset(lnsc[:, 7:8], EPS)
    vc2.inc(VE.drain())

    wkv_marks = []
    idx = 0
    for dsrc, dsth, dstl in ((d_wk, wk_h, wk_l), (d_wv, wv_h, wv_l)):
        for k in range(KCH):
            slot = idx % 2
            sl = wabs[:, slot * 1024:slot * 1024 + 512]
            if idx >= 2:
                SY.wait_ge(vc2.h, wkv_marks[idx - 2])
            s_wab[slot].inc(SY.dma_start(out=sl, in_=dsrc[k * 128:(k + 1) * 128, :]), 16)
            s_wab[slot].wait(SE)
            cols = slice(k * 512, (k + 1) * 512)
            sc2.inc(SE.copy(dsth[:, cols], sl))
            sc2.wait(VE)
            vc2.inc(VE.tensor_tensor(dstl[:, cols], sl, dsth[:, cols], op=OP.subtract))
            wkv_marks.append(vc2.n)
            idx += 1
    post_wkv_v = vc2.n
    if stage_done():
        return nc

    scale_tiles = []
    for wi, d in enumerate((d_wgT, d_wuT, d_wdT)):
        rows, cols = (DM, FI) if wi < 2 else (FI, DM)
        for r in range(rows // 128):
            for c in range(cols // 1024):
                scale_tiles.append((wi, d[r * 128:(r + 1) * 128,
                                          c * 1024:(c + 1) * 1024]))
    assert len(scale_tiles) == 384
    sp_state = {"i": 0, "marks": [], "vmarks": []}

    def emit_scale_tiles(n):
        for _ in range(n):
            i = sp_state["i"]
            if i >= len(scale_tiles):
                return
            wi, src = scale_tiles[i]
            slot = i % 2
            sl = wabs[:, slot * 1024:(slot + 1) * 1024]
            if i >= 2:
                SE.wait_ge(sc2.h, sp_state["marks"][i - 2])
            else:
                SE.wait_ge(vc2.h, post_wkv_v)
            s_wab[slot].inc(SE.dma_start(out=sl, in_=src), 16)
            s_wab[slot].wait(SE)
            sc2.inc(SE.activation(sl, sl, AF.Abs, accum_out=wacc[:, i:i + 1]))
            sp_state["marks"].append(sc2.n)
            sp_state["i"] += 1

    def q_build2(phase):
        qctx = ExitStack()
        lnq_h = qctx.enter_context(nc.sbuf_tensor(f"lnq_h{phase}", [128, KCH * 512], BF16))
        lnq_l = qctx.enter_context(nc.sbuf_tensor(f"lnq_l{phase}", [128, KCH * 512], BF16))
        qps_ctx = ExitStack()
        qtr = [qps_ctx.enter_context(
            nc.psum_tensor(f"qtr{phase}_{i}", [128, 1024], BF16)) for i in range(2)]
        for sub in range(4):
            r0 = phase * QB + sub * 128

            def emitq(eng, bank, kbase, take, is_lo, sub=sub):
                dstb = lnq_l if is_lo else lnq_h
                for i in range(take):
                    col = (kbase + i) * 512 + sub * 128
                    if eng is VE:
                        VE.tensor_copy(dstb[:, col:col + 128],
                                       bank[:, i * 128:(i + 1) * 128])
                    else:
                        SE.copy(dstb[:, col:col + 128],
                                bank[:, i * 128:(i + 1) * 128])
            ln1_subtile(d_xq[r0:r0 + 128, :], qtr, [tr_done[0], tr_done[1]],
                        [tr_free[0], tr_free[1]], ln_h, ln_l, emitq)
        bar()
        qps_ctx.close()
        wctx = ExitStack()
        wqf = wctx.enter_context(nc.sbuf_tensor(f"wqf{phase}", [128, 2 * 768], F32))
        wqh = wctx.enter_context(nc.sbuf_tensor(f"wqh{phase}", [128, 2 * 768], BF16))
        wql = wctx.enter_context(nc.sbuf_tensor(f"wql{phase}", [128, 2 * 768], BF16))
        qps_ctx2 = ExitStack()
        qps = [qps_ctx2.enter_context(
            nc.psum_tensor(f"qps{phase}_{i}", [128, 512], F32)) for i in range(6)]
        spl_v.wait(TE)
        spl_s.wait(TE)
        li = 0
        wq_marks = []
        heads_done = 0
        while heads_done < H:
            nh = min(6, H - heads_done)
            for k in range(KCH):
                slot = li % 2
                wsl = wqf[:, slot * 768:slot * 768 + nh * 128]
                if li >= 2:
                    SY.wait_ge(t2v.h, wq_marks[li - 2])
                s_wq[slot].inc(SY.dma_start(
                    out=wsl,
                    in_=d_wq[k * 128:(k + 1) * 128,
                             heads_done * 128:(heads_done + nh) * 128]), 16)
                s_wq[slot].wait(SE)
                hcols = slice(slot * 768, slot * 768 + nh * 128)
                spl_s.inc(SE.copy(wqh[:, hcols], wsl))
                spl_s.wait(VE)
                spl_v.inc(VE.tensor_tensor(wql[:, hcols], wsl, wqh[:, hcols], op=OP.subtract))
                spl_v.wait(TE)
                spl_s.wait(TE)
                for hh in range(nh):
                    hc = slot * 768 + hh * 128
                    wh = wqh[:, hc:hc + 128]
                    wl = wql[:, hc:hc + 128]
                    TE.matmul(qps[hh][:, :], wh, lnq_h[:, k * 512:(k + 1) * 512],
                              start=(k == 0), stop=False)
                    TE.matmul(qps[hh][:, :], wh, lnq_l[:, k * 512:(k + 1) * 512],
                              start=False, stop=False)
                    last = TE.matmul(qps[hh][:, :], wl, lnq_h[:, k * 512:(k + 1) * 512],
                                     start=False, stop=(k == KCH - 1))
                t2v.inc(last)
                wq_marks.append(t2v.n)
                li += 1
            t2v.wait(SE)
            last = None
            for hh in range(nh):
                cols = slice((heads_done + hh) * 512, (heads_done + hh + 1) * 512)
                last = SE.copy(q_h[:, cols], qps[hh][:, :])
            spl_s.inc(last)
            spl_s.wait(VE)
            t2v.wait(VE)
            last = None
            for hh in range(nh):
                cols = slice((heads_done + hh) * 512, (heads_done + hh + 1) * 512)
                last = VE.tensor_tensor(q_l[:, cols], qps[hh][:, :], q_h[:, cols],
                                        op=OP.subtract)
            spl_v.inc(last)
            v2t.inc(VE.engine_nop())
            v2t.wait(TE)
            heads_done += nh
        bar()
        qps_ctx2.close()
        wctx.close()
        qctx.close()
        return spl_v.n, spl_s.n

    def kv_chunk(phase, step, kv_bufs, aps):
        (k_h, k_l, v_h, v_l) = kv_bufs
        (sc_ps, pt_ps, pa_ps) = aps
        base = (0 if phase == 0 else ASTEPS) + step
        TE.wait_ge(vc2.h, vc2.n)
        for tt in range(4):
            r0 = base * CH + tt * 128

            def emitk(eng, bank, kbase, take, is_lo):
                dstb = lnf_l if is_lo else lnf_h
                if eng is VE:
                    VE.tensor_copy(dstb[:, kbase * 128:(kbase + take) * 128],
                                   bank[:, :take * 128])
                else:
                    SE.copy(dstb[:, kbase * 128:(kbase + take) * 128],
                            bank[:, :take * 128])
            ln1_subtile(d_xkv[r0:r0 + 128, :], pt_ps, [a_pt[0], a_pt[0]],
                        [a_pe[0], a_pe[0]], ln_h, ln_l, emitk)
            spl_v.wait(TE)
            spl_s.wait(TE)
            for g in range(KVH):
                bi = g % 2
                pa_free[bi].wait(TE)
                for k in range(KCH):
                    wcol = slice(k * 512 + g * 128, k * 512 + (g + 1) * 128)
                    lncol = slice(k * 128, (k + 1) * 128)
                    TE.matmul(pa_ps[bi][:, :128], wk_h[:, wcol], lnf_h[:, lncol],
                              start=(k == 0), stop=False)
                    TE.matmul(pa_ps[bi][:, :128], wk_h[:, wcol], lnf_l[:, lncol],
                              start=False, stop=False)
                    last = TE.matmul(pa_ps[bi][:, :128], wk_l[:, wcol], lnf_h[:, lncol],
                                     start=False, stop=(k == KCH - 1))
                pa_done[bi].inc(last)
                kcols = slice(g * 512 + tt * 128, g * 512 + (tt + 1) * 128)
                pa_done[bi].wait(SE)
                spl_s.inc(SE.copy(k_h[:, kcols], pa_ps[bi][:, :128]))
                spl_s.wait(VE)
                spl_v.inc(VE.tensor_tensor(k_l[:, kcols], pa_ps[bi][:, :128],
                                           k_h[:, kcols], op=OP.subtract))
                pa_free[bi].inc(VE.engine_nop())
            bi = tt % 2
            pa_free[bi].wait(TE)
            for k in range(KCH):
                lncol = slice(k * 128, (k + 1) * 128)
                wcols = slice(k * 512, (k + 1) * 512)
                TE.matmul(pa_ps[bi][:, :], lnf_h[:, lncol], wv_h[:, wcols],
                          start=(k == 0), stop=False)
                TE.matmul(pa_ps[bi][:, :], lnf_h[:, lncol], wv_l[:, wcols],
                          start=False, stop=False)
                last = TE.matmul(pa_ps[bi][:, :], lnf_l[:, lncol], wv_h[:, wcols],
                                 start=False, stop=(k == KCH - 1))
            pa_done[bi].inc(last)
            vcols = slice(tt * 512, (tt + 1) * 512)
            pa_done[bi].wait(SE)
            spl_s.inc(SE.copy(v_h[:, vcols], pa_ps[bi][:, :]))
            spl_s.wait(VE)
            spl_v.inc(VE.tensor_tensor(v_l[:, vcols], pa_ps[bi][:, :], v_h[:, vcols],
                                       op=OP.subtract))
            pa_free[bi].inc(VE.engine_nop())
        return spl_v.n, spl_s.n

    ALVL = int(os.environ.get("KB_ALVL", "9"))

    AMARKS = {"sc": {}, "cm": {}, "vb": {}, "sp": {}, "ph": {}, "pl": {},
              "tr": {}, "ev": {}, "av": {}, "o": {}, "sd": {}}

    def attend_step(phase, step, kv_bufs, aps, o_acc, kv_v, kv_s, q_v, q_s):
        (k_h, k_l, v_h, v_l) = kv_bufs
        (sc_ps, pt_ps, pa_ps) = aps
        NB = len(sc_ps)
        NC = len(pt_ps)
        par_in = step % 2
        par_out = (step + 1) % 2
        gcol = (0 if phase == 0 else ASTEPS) + step
        gate_col = gates_s[:, gcol:gcol + 1]
        n = 64
        u0 = unit_ctr[0]
        unit_ctr[0] += n
        M = AMARKS
        ln_ev_mark = (a_pe[0], a_pe[0].n)
        first = [True]

        def wai(eng, mark):
            if mark is not None:
                eng.wait_ge(mark[0].h, mark[1])

        def st_scores(i):
            u = u0 + i
            b = u % NB
            q, h = divmod(i, H)
            g = h // NREP
            wai(TE, M["o"].get(u - NB))
            if first[0]:
                TE.wait_ge(spl_v.h, max(kv_v, q_v))
                TE.wait_ge(spl_s.h, max(kv_s, q_s))
                first[0] = False
            qh = q_h[:, h * 512 + q * 128: h * 512 + (q + 1) * 128]
            ql = q_l[:, h * 512 + q * 128: h * 512 + (q + 1) * 128]
            TE.matmul(sc_ps[b][:, :], qh, k_h[:, g * 512:(g + 1) * 512],
                      start=True, stop=False)
            TE.matmul(sc_ps[b][:, :], qh, k_l[:, g * 512:(g + 1) * 512],
                      start=False, stop=False)
            last = TE.matmul(sc_ps[b][:, :], ql, k_h[:, g * 512:(g + 1) * 512],
                             start=False, stop=True)
            a_sc[0].inc(last)
            M["sc"][u] = (a_sc[0], a_sc[0].n)

        def st_stats(i):
            u = u0 + i
            b = u % NB
            q, h = divmod(i, H)
            uc = i
            ucs = slice(uc, uc + 1)
            wai(VE, M["sc"][u])
            wai(VE, M["vb"].get(u - 64))
            if step == 0:
                VE.tensor_tensor(sc_ps[b][:, :], sc_ps[b][:, :],
                                 masks_s[:, q * CH:(q + 1) * CH], op=OP.add)
                VE.drain()
            VE.reduce_max(cmx[:, ucs], sc_ps[b][:, :], axis=AX)
            VE.drain()
            mi = m_run[:, par_in * 64 + uc: par_in * 64 + uc + 1]
            mo = m_run[:, par_out * 64 + uc: par_out * 64 + uc + 1]
            a_sc[1].inc(VE.scalar_tensor_tensor(mo, cmx[:, ucs], gate_col, mi,
                                                op0=OP.add, op1=OP.max))
            M["cm"][u] = (a_sc[1], a_sc[1].n)

        def st_mbias(i):
            u = u0 + i
            uc = i
            ucs = slice(uc, uc + 1)
            mo = m_run[:, par_out * 64 + uc: par_out * 64 + uc + 1]
            wai(GP, M["cm"][u])
            GP.tensor_scalar(bi_p[:, ucs], mo, -SCALE, gate_col,
                             op0=OP.mult, op1=OP.add)
            a_vb.inc(GP.tensor_scalar(bi_a[:, ucs], mo, -SCALE, None, op0=OP.mult))
            M["vb"][u] = (a_vb, a_vb.n)

        def st_exp(i):
            u = u0 + i
            b = u % NB
            ps = u % 3
            uc = i
            ucs = slice(uc, uc + 1)
            mi = m_run[:, par_in * 64 + uc: par_in * 64 + uc + 1]
            wai(SE, M["vb"][u])
            wai(SE, M["tr"].get(u - 3))
            SE.activation(alph[:, ucs], mi, AF.Exp, bias=bi_a[:, ucs], scale=SCALE)
            a_ph.inc(SE.activation(p_h[:, ps * CH:(ps + 1) * CH], sc_ps[b][:, :],
                                   AF.Exp, bias=bi_p[:, ucs], scale=SCALE))
            M["ph"][u] = (a_ph, a_ph.n)
            a_sp.inc(SE.activation(p_f[:, ps * CH:(ps + 1) * CH], sc_ps[b][:, :],
                                   AF.Exp, bias=bi_p[:, ucs], scale=SCALE,
                                   accum_out=csum[:, ucs]))
            M["sp"][u] = (a_sp, a_sp.n)

        def st_plo(i):
            u = u0 + i
            ps = u % 3
            uc = i
            ucs = slice(uc, uc + 1)
            wai(GP, M["ph"][u])
            wai(GP, M["sp"][u])
            a_pl.inc(GP.tensor_tensor(p_l[:, ps * CH:(ps + 1) * CH],
                                      p_f[:, ps * CH:(ps + 1) * CH],
                                      p_h[:, ps * CH:(ps + 1) * CH],
                                      op=OP.subtract))
            M["pl"][u] = (a_pl, a_pl.n)
            wai(VE, M["sp"][u])
            wai(VE, M["sd"].get(u - 64))
            vc2.inc(VE.scalar_tensor_tensor(
                sden[:, par_out * 64 + uc: par_out * 64 + uc + 1],
                sden[:, par_in * 64 + uc: par_in * 64 + uc + 1],
                alph[:, ucs], csum[:, ucs], op0=OP.mult, op1=OP.add))
            M["sd"][u] = (vc2, vc2.n)

        def st_trans(i):
            u = u0 + i
            c = u % NC
            ps = u % 3
            wai(TE, M["ev"].get(u - NC))
            wai(TE, ln_ev_mark)
            wai(TE, M["ph"][u])
            wai(TE, M["pl"][u])
            for k in range(4):
                TE.transpose(pt_ps[c][:, k * 128:(k + 1) * 128],
                             p_h[:, ps * CH + k * 128: ps * CH + (k + 1) * 128],
                             ident_s[:, :])
            last = None
            for k in range(4):
                last = TE.transpose(pt_ps[c][:, 512 + k * 128: 512 + (k + 1) * 128],
                                    p_l[:, ps * CH + k * 128: ps * CH + (k + 1) * 128],
                                    ident_s[:, :])
            a_pt[0].inc(last)
            M["tr"][u] = (a_pt[0], a_pt[0].n)

        def st_evac(i):
            u = u0 + i
            c = u % NC
            ps = u % 3
            eng = SE if c == 1 else VE
            wai(eng, M["tr"][u])
            wai(eng, M["av"].get(u - 3))
            if eng is SE:
                a_pe[0].inc(SE.copy(pt_hl[:, ps * 1024:(ps + 1) * 1024],
                                    pt_ps[c][:, :]))
            else:
                a_pe[0].inc(VE.tensor_copy(pt_hl[:, ps * 1024:(ps + 1) * 1024],
                                           pt_ps[c][:, :]))
            M["ev"][u] = (a_pe[0], a_pe[0].n)

        def st_av(i):
            u = u0 + i
            b = u % NB
            ps = u % 3
            q, h = divmod(i, H)
            g = h // NREP
            wai(TE, M["ev"][u])
            wai(TE, M["sp"][u])
            last = None
            for k in range(4):
                pth = pt_hl[:, ps * 1024 + k * 128: ps * 1024 + (k + 1) * 128]
                ptl = pt_hl[:, ps * 1024 + 512 + k * 128: ps * 1024 + 512 + (k + 1) * 128]
                vh = v_h[:, k * 512 + g * 128: k * 512 + (g + 1) * 128]
                vl = v_l[:, k * 512 + g * 128: k * 512 + (g + 1) * 128]
                TE.matmul(sc_ps[b][:, :128], pth, vh, start=(k == 0), stop=False)
                TE.matmul(sc_ps[b][:, :128], pth, vl, start=False, stop=False)
                last = TE.matmul(sc_ps[b][:, :128], ptl, vh, start=False,
                                 stop=(k == 3))
            t2v.inc(last)
            M["av"][u] = (t2v, t2v.n)

        def st_oupd(i):
            u = u0 + i
            b = u % NB
            q, h = divmod(i, H)
            uc = i
            ucs = slice(uc, uc + 1)
            wai(VE, M["av"][u])
            osl = o_acc[:, q * DM + h * 128: q * DM + (h + 1) * 128]
            vc2.inc(VE.scalar_tensor_tensor(osl, osl, alph[:, ucs],
                                            sc_ps[b][:, :128],
                                            op0=OP.mult, op1=OP.add))
            M["o"][u] = (vc2, vc2.n)

        for cyc in range(n + 7):
            for off, fn in ((6, st_oupd), (5, st_evac), (5, st_av), (4, st_plo),
                            (4, st_trans), (3, st_exp), (2, st_mbias), (1, st_stats),
                            (0, st_scores)):
                j = cyc - off
                if 0 <= j < n:
                    fn(j)

    def wo_phase(phase, nsteps, o_acc):
        par_fin = nsteps % 2
        vc2.wait(VE)
        VE.reciprocal(rden[:, :64], sden[:, par_fin * 64: par_fin * 64 + 64])
        vc2.inc(VE.drain())
        otr_ctx = ExitStack()
        otr = [otr_ctx.enter_context(
            nc.psum_tensor(f"otr{phase}_{i}", [128, 1024], BF16)) for i in range(2)]
        tr_marks = []
        for q in range(4):
            for h in range(H):
                i = q * H + h
                uc = i
                osl = o_acc[:, q * DM + h * 128: q * DM + (h + 1) * 128]
                pslot = i % 4
                pcol = slice(pslot * 128, pslot * 128 + 128)
                if i >= 4:
                    VE.wait_ge(t2v.h, tr_marks[i - 4])
                VE.tensor_scalar(p_h[:, pcol], osl, rden[:, uc:uc + 1], None,
                                 op0=OP.mult)
                VE.drain()
                VE.scalar_tensor_tensor(p_l[:, pcol], osl, rden[:, uc:uc + 1],
                                        p_h[:, pcol], op0=OP.mult, op1=OP.subtract)
                v2t.inc(VE.drain())
                v2t.wait(TE)
                bi = i % 2
                tr_free[bi].wait(TE)
                TE.transpose(otr[bi][:, :128], p_h[:, pcol], ident_s[:, :])
                last = TE.transpose(otr[bi][:, 128:256], p_l[:, pcol], ident_s[:, :])
                tr_done[bi].inc(last)
                t2v.inc(TE.drain())
                tr_marks.append(t2v.n)
                eng = VE if bi == 0 else SE
                tr_done[bi].wait(eng)
                dcol = slice(h * 512 + q * 128, h * 512 + (q + 1) * 128)
                if eng is VE:
                    VE.tensor_copy(q_h[:, dcol], otr[bi][:, :128])
                    VE.tensor_copy(q_l[:, dcol], otr[bi][:, 128:256])
                    vc2.inc(VE.drain())
                    tr_free[bi].inc(VE.engine_nop())
                else:
                    SE.copy(q_h[:, dcol], otr[bi][:, :128])
                    SE.copy(q_l[:, dcol], otr[bi][:, 128:256])
                    sc2.inc(SE.drain())
                    tr_free[bi].inc(SE.drain())
        vc2.wait(TE)
        sc2.wait(TE)
        bar()
        otr_ctx.close()
        wctx = ExitStack()
        wqf = wctx.enter_context(nc.sbuf_tensor(f"wof{phase}", [128, 2 * 1024], F32))
        wqh = wctx.enter_context(nc.sbuf_tensor(f"woh{phase}", [128, 2 * 1024], BF16))
        wql = wctx.enter_context(nc.sbuf_tensor(f"wol{phase}", [128, 2 * 1024], BF16))
        wops_ctx = ExitStack()
        wops = [wops_ctx.enter_context(
            nc.psum_tensor(f"wops{phase}_{i}", [128, 512], F32)) for i in range(8)]
        wo_marks = []
        li = 0
        for ocg in range(2):
            for k in range(KCH):
                slot = li % 2
                wsl = wqf[:, slot * 1024:(slot + 1) * 1024]
                if li >= 2:
                    SY.wait_ge(t2v.h, wo_marks[li - 2])
                s_wq[slot].inc(SY.dma_start(
                    out=wsl,
                    in_=d_wo[k * 128:(k + 1) * 128, ocg * 1024:(ocg + 1) * 1024]), 16)
                s_wq[slot].wait(SE)
                hcols = slice(slot * 1024, (slot + 1) * 1024)
                SE.copy(wqh[:, hcols], wsl)
                sc2.inc(SE.drain())
                sc2.wait(VE)
                VE.tensor_tensor(wql[:, hcols], wsl, wqh[:, hcols], op=OP.subtract)
                vc2.inc(VE.drain())
                vc2.wait(TE)
                sc2.wait(TE)
                for tt in range(4):
                    for oc in range(2):
                        pb = wops[tt * 2 + oc]
                        wh = wqh[:, slot * 1024 + oc * 512: slot * 1024 + (oc + 1) * 512]
                        wl = wql[:, slot * 1024 + oc * 512: slot * 1024 + (oc + 1) * 512]
                        lh = q_h[:, k * 512 + tt * 128: k * 512 + (tt + 1) * 128]
                        ll = q_l[:, k * 512 + tt * 128: k * 512 + (tt + 1) * 128]
                        TE.matmul(pb[:, :], lh, wh, start=(k == 0), stop=False)
                        TE.matmul(pb[:, :], ll, wh, start=False, stop=False)
                        last = TE.matmul(pb[:, :], lh, wl, start=False, stop=(k == KCH - 1))
                t2v.inc(last)
                wo_marks.append(t2v.n)
                li += 1
            t2v.wait(VE)
            for tt in range(4):
                slot = tile_ctr[0] % 2
                tile_ctr[0] += 1
                sl = x_sb[:, slot * DM:(slot + 1) * DM]
                xf = s_xf[slot]
                xf.wait(SY)
                r0 = phase * QB + tt * 128
                s_x[slot].inc(SY.dma_start(out=sl[:, :1024],
                                           in_=d_xq[r0:r0 + 128,
                                                    ocg * 1024:(ocg + 1) * 1024]), 16)
                s_x[slot].wait(VE)
                for oc in range(2):
                    VE.tensor_tensor(sl[:, 1024 + oc * 512: 1024 + (oc + 1) * 512],
                                     wops[tt * 2 + oc][:, :],
                                     sl[:, oc * 512:(oc + 1) * 512], op=OP.add)
                vc2.inc(VE.drain())
                v2t.inc(VE.engine_nop())
                vc2.wait(SY)
                st_x[slot].inc(SY.dma_start(
                    out=d_rsp[r0:r0 + 128, ocg * 1024:(ocg + 1) * 1024],
                    in_=sl[:, 1024:2048]), 16)
                st_x[slot].wait(SY)
                xf.inc(SY.nop(), 1)
            v2t.wait(TE)
        bar()
        wops_ctx.close()
        wctx.close()

    for phase, nsteps in ((0, ASTEPS), (1, BSTEPS)):
        VE.memset(m_run[:, :64], NEG)
        VE.memset(sden[:, :64], 0.0)
        vc2.inc(VE.drain())
        q_v, q_s = q_build2(phase)
        if stage_done():
            return nc
        o_ctx = ExitStack()
        o_acc = o_ctx.enter_context(nc.sbuf_tensor(f"o_acc{phase}", [128, 4 * DM], F32))
        for qq in range(4):
            VE.memset(o_acc[:, qq * DM:(qq + 1) * DM], 0.0)
        vc2.inc(VE.drain())
        kv_ctx = ExitStack()
        k_h = kv_ctx.enter_context(nc.sbuf_tensor(f"k_h{phase}", [128, KVH * 512], BF16))
        k_l = kv_ctx.enter_context(nc.sbuf_tensor(f"k_l{phase}", [128, KVH * 512], BF16))
        v_h = kv_ctx.enter_context(nc.sbuf_tensor(f"v_h{phase}", [128, 4 * 512], BF16))
        v_l = kv_ctx.enter_context(nc.sbuf_tensor(f"v_l{phase}", [128, 4 * 512], BF16))
        aps_ctx = ExitStack()
        sc_ps = [aps_ctx.enter_context(nc.psum_tensor(f"sc{phase}_{i}", [128, 512], F32))
                 for i in range(6)]
        pt_ps = [aps_ctx.enter_context(nc.psum_tensor(f"pt{phase}_{i}", [128, 1024], BF16))
                 for i in range(2)]
        pa_ps = [sc_ps[0], sc_ps[1]]
        kv_bufs = (k_h, k_l, v_h, v_l)
        aps = (sc_ps, pt_ps, pa_ps)
        for step in range(nsteps):
            kv_v, kv_s = kv_chunk(phase, step, kv_bufs, aps)
            if not os.environ.get("KB_NO_ATTEND"):
                attend_step(phase, step, kv_bufs, aps, o_acc, kv_v, kv_s, q_v, q_s)
            if not os.environ.get("KB_NO_SCALE"):
                emit_scale_tiles(32)
        if stage_done():
            return nc
        bar()
        aps_ctx.close()
        kv_ctx.close()
        wo_phase(phase, nsteps, o_acc)
        if stage_done():
            return nc
        bar()
        o_ctx.close()
    emit_scale_tiles(384)

    sc2.wait(VE)
    for wi in range(3):
        VE.reduce_sum(asum[:, wi:wi + 1], wacc[:, wi * 128:(wi + 1) * 128], axis=AX)
    VE.drain()
    VE.tensor_copy(wabs_t[:, 0:3], asum[:, 0:3])
    VE.drain()
    VE.tensor_tensor(asum[:, 4:7], asum[:, 0:3], wabs_t[:, 0:3], op=OP.subtract)
    VE.drain()
    VE.tensor_copy(wabs_t[:, 4:7], asum[:, 4:7])
    VE.drain()
    VE.memset(wabs_t[:, 3:4], 0.0)
    VE.memset(wabs_t[:, 7:8], 0.0)
    vc2.inc(VE.drain())
    v2t.inc(VE.engine_nop())
    v2t.wait(TE)
    gp_done.wait(TE)
    scp_ctx = ExitStack()
    scp = scp_ctx.enter_context(nc.psum_tensor("scp", [128, 8], F32))
    last = TE.matmul(scp[:, :], ones_b[:, :], wabs_t[:, :8], start=True, stop=True)
    t2v.inc(last)
    t2v.wait(VE)
    VE.tensor_copy(wsml[:, 8:16], scp[:, 0:8])
    VE.drain()
    VE.tensor_tensor(wsml[:, 0:3], wsml[:, 8:11], wsml[:, 12:15], op=OP.add)
    VE.drain()
    VE.tensor_scalar(wsml[:, 0:3], wsml[:, 0:3], float(2.0 ** -24), EPS,
                     op0=OP.mult, op1=OP.add)
    VE.drain()
    VE.tensor_scalar(wsml[:, 4:7], wsml[:, 0:3], 0.5, None, op0=OP.mult)
    vc2.inc(VE.drain())
    v2t.inc(VE.engine_nop())
    v2t.wait(TE)
    bar()
    scp_ctx.close()
    attn_ctx.close()
    if stage_done():
        return nc

    WSG, WSU, WSD = 0, 1, 2
    ffn_ctx = ExitStack()
    fst = ffn_ctx.enter_context(nc.sbuf_tensor("fst", [128, 48], F32))
    fscr_ctx = ExitStack()
    f32a = fscr_ctx.enter_context(nc.sbuf_tensor("f32a", [128, 2 * DM], F32))
    f32b = fscr_ctx.enter_context(nc.sbuf_tensor("f32b", [128, 2 * DM], F32))
    bf_a = fscr_ctx.enter_context(nc.sbuf_tensor("bf_a", [128, 2 * DM], BF16))
    i32a = fscr_ctx.enter_context(nc.sbuf_tensor("i32a", [128, DM], I32))

    def quant_act_tile(src_f32, rr_ap, dst_bf, W):
        VE.tensor_scalar(i32a[:, :W], src_f32, rr_ap, None, op0=OP.mult)
        VE.drain()
        VE.tensor_copy(f32b[:, :W], i32a[:, :W])
        VE.drain()
        VE.scalar_tensor_tensor(f32b[:, DM:DM + W], src_f32, rr_ap, f32b[:, :W],
                                op0=OP.mult, op1=OP.subtract)
        VE.drain()
        VE.tensor_scalar(bf_a[:, :W], f32b[:, DM:DM + W], 0.5, None, op0=OP.is_ge)
        VE.tensor_scalar(bf_a[:, DM:DM + W], f32b[:, DM:DM + W], -1.0, 0.5,
                         op0=OP.mult, op1=OP.is_ge)
        VE.drain()
        VE.tensor_tensor(f32b[:, DM:DM + W], bf_a[:, :W], bf_a[:, DM:DM + W],
                         op=OP.subtract)
        VE.drain()
        VE.tensor_tensor(dst_bf, f32b[:, :W], f32b[:, DM:DM + W], op=OP.add)
        vc2.inc(VE.drain())

    xq_ctx = ExitStack()
    xqfm = xq_ctx.enter_context(nc.sbuf_tensor("xqfm", [128, KCH * NTOK], BF16))
    f1_ps = ExitStack()
    f1tr = [f1_ps.enter_context(nc.psum_tensor("f1tr0", [128, 1024], BF16)),
            f1_ps.enter_context(nc.psum_tensor("f1tr1", [128, 1024], BF16))]
    f1_marks = []
    f1_tr_marks = []
    for tt in range(8):
        r0 = tt * 128
        slot = tt % 2
        sl = f32a[:, slot * DM:(slot + 1) * DM]
        if tt >= 2:
            SY.wait_ge(vc2.h, f1_marks[tt - 2])
        s_fa[slot].inc(SY.dma_start(out=sl, in_=d_rsp[r0:r0 + 128, :]), 16)
        s_fa[slot].wait(VE)
        for c in range(4):
            VE.bn_stats(bn6[:, 6 * c:6 * (c + 1)], sl[:, 512 * c:512 * (c + 1)])
        VE.drain()
        VE.bn_aggr(bn2[:, :], bn6[:, :])
        VE.drain()
        VE.tensor_scalar(lnsc[:, 0:1], bn2[:, 0:1], -1.0, None, op0=OP.mult)
        vc2.inc(VE.drain())
        vc2.wait(SE)
        SE.activation(lnsc[:, 1:2], bn2[:, 1:2], AF.Sqrt, bias=lnsc[:, 7:8], scale=1.0)
        sc2.inc(SE.drain())
        sc2.wait(VE)
        VE.reciprocal(lnsc[:, 2:3], lnsc[:, 1:2])
        VE.drain()
        VE.tensor_scalar(sl, sl, lnsc[:, 0:1], None, op0=OP.add)
        VE.drain()
        VE.reduce_max(lnsc[:, 4:5], sl, axis=AX, apply_absolute_value=True)
        VE.drain()
        VE.scalar_tensor_tensor(fst[:, tt:tt + 1], lnsc[:, 4:5],
                                float(1.0 / 127.0), lnsc[:, 2:3],
                                op0=OP.mult, op1=OP.mult)
        VE.drain()
        VE.tensor_scalar(fst[:, tt:tt + 1], fst[:, tt:tt + 1], EPS, None,
                         op0=OP.add)
        VE.drain()
        VE.reciprocal(lnsc[:, 4:5], fst[:, tt:tt + 1])
        VE.drain()
        VE.tensor_tensor(lnsc[:, 4:5], lnsc[:, 4:5], lnsc[:, 2:3], op=OP.mult)
        VE.drain()
        if tt >= 1:
            VE.wait_ge(t2v.h, f1_tr_marks[tt - 1])
        quant_act_tile(sl, lnsc[:, 4:5], bf_a[:, :DM], DM)
        f1_marks.append(vc2.n)
        vc2.wait(TE)
        done = 0
        while done < KCH:
            bi = (done // 8) % 2
            take = min(8, KCH - done)
            tr_free[bi].wait(TE)
            last = None
            for i in range(take):
                last = TE.transpose(f1tr[bi][:, i * 128:(i + 1) * 128],
                                    bf_a[:, (done + i) * 128:(done + i + 1) * 128],
                                    ident_s[:, :])
            tr_done[bi].inc(last)
            eng = VE if bi == 0 else SE
            tr_done[bi].wait(eng)
            for i in range(take):
                col = (done + i) * NTOK + tt * 128
                if eng is VE:
                    VE.tensor_copy(xqfm[:, col:col + 128],
                                   f1tr[bi][:, i * 128:(i + 1) * 128])
                else:
                    SE.copy(xqfm[:, col:col + 128],
                            f1tr[bi][:, i * 128:(i + 1) * 128])
            if eng is VE:
                vc2.inc(VE.drain())
                tr_free[bi].inc(VE.engine_nop())
            else:
                sc2.inc(SE.drain())
                tr_free[bi].inc(SE.drain())
            done += take
        t2v.inc(TE.drain())
        f1_tr_marks.append(t2v.n)
    vc2.wait(VE)
    VE.tensor_scalar(fst[:, 16:24], fst[:, 0:8], wsml[:, 0:1], None, op0=OP.mult)
    VE.tensor_scalar(fst[:, 24:32], fst[:, 0:8], wsml[:, 1:2], None, op0=OP.mult)
    VE.memset(fst[:, 32:40], 0.0)
    vc2.inc(VE.drain())
    if stage_done():
        return nc
    bar()
    f1_ps.close()

    f2_ctx = ExitStack()
    wgq = f2_ctx.enter_context(nc.sbuf_tensor("wgq", [128, KCH * 512], BF16))
    wuq = f2_ctx.enter_context(nc.sbuf_tensor("wuq", [128, KCH * 512], BF16))
    g_sb = f2_ctx.enter_context(nc.sbuf_tensor("g_sb", [128, 8 * 512], F32))
    h_sb = f2_ctx.enter_context(nc.sbuf_tensor("h_sb", [128, 2 * 512], F32))
    f2_ps = ExitStack()
    fps = [f2_ps.enter_context(nc.psum_tensor(f"fps{i}", [128, 512], F32))
           for i in range(8)]
    fps_free = [S(nc, f"fpsf{i}") for i in range(8)]
    fps_done = [0] * 8
    fq_marks = []
    fq_li = [0]
    h_st_marks = []

    def quant_w_tiles(dsrc, dst, wi, oc, mm_guard):
        for k in range(KCH):
            li = fq_li[0]
            fq_li[0] += 1
            slot = li % 2
            sl = f32a[:, slot * DM: slot * DM + 512]
            if li >= 2:
                SY.wait_ge(vc2.h, fq_marks[li - 2])
            s_fa[slot].inc(SY.dma_start(
                out=sl, in_=dsrc[k * 128:(k + 1) * 128,
                                 oc * 512:(oc + 1) * 512]), 16)
            s_fa[slot].wait(VE)
            if k == 0:
                VE.wait_ge(t2v.h, mm_guard)
            bslot = li % 2
            bc2 = slice(bslot * 1024 + 512, bslot * 1024 + 1024)
            VE.tensor_scalar(bf_a[:, bc2], sl, -1.0, wsml[:, 4 + wi:5 + wi],
                             op0=OP.mult, op1=OP.is_ge)
            VE.drain()
            vc2.inc(VE.scalar_tensor_tensor(dst[:, k * 512:(k + 1) * 512], sl,
                                            wsml[:, 4 + wi:5 + wi], bf_a[:, bc2],
                                            op0=OP.is_ge, op1=OP.subtract))
            fq_marks.append(vc2.n)

    mm_guard = [t2v.n]
    for oc in range(FI // 512):
        quant_w_tiles(d_wgT, wgq, WSG, oc, mm_guard[0])
        quant_w_tiles(d_wuT, wuq, WSU, oc, mm_guard[0])
        vc2.wait(TE)
        for tt in range(8):
            pb = fps[tt % 4]
            fps_free[tt % 4].wait(TE)
            for k in range(KCH):
                last = TE.matmul(pb[:, :],
                                 xqfm[:, k * NTOK + tt * 128: k * NTOK + (tt + 1) * 128],
                                 wgq[:, k * 512:(k + 1) * 512],
                                 start=(k == 0), stop=(k == KCH - 1))
            t2v.inc(last)
            fps_done[tt % 4] = t2v.n
            SE.wait_ge(t2v.h, fps_done[tt % 4])
            SE.activation(g_sb[:, tt * 512:(tt + 1) * 512], pb[:, :], AF.Silu,
                          scale=fst[:, 16 + tt:17 + tt])
            sc2.inc(SE.drain())
            fps_free[tt % 4].inc(SE.drain())
        for tt in range(8):
            pb = fps[4 + tt % 4]
            fps_free[4 + tt % 4].wait(TE)
            for k in range(KCH):
                last = TE.matmul(pb[:, :],
                                 xqfm[:, k * NTOK + tt * 128: k * NTOK + (tt + 1) * 128],
                                 wuq[:, k * 512:(k + 1) * 512],
                                 start=(k == 0), stop=(k == KCH - 1))
            t2v.inc(last)
            fps_done[4 + tt % 4] = t2v.n
            hidx = oc * 8 + tt
            hs = hidx % 2
            hsl = h_sb[:, hs * 512:(hs + 1) * 512]
            VE.wait_ge(t2v.h, fps_done[4 + tt % 4])
            sc2.wait(VE)
            if hidx >= 2:
                st_h[hs].wait(VE)
            VE.scalar_tensor_tensor(hsl, pb[:, :], fst[:, 24 + tt:25 + tt],
                                    g_sb[:, tt * 512:(tt + 1) * 512],
                                    op0=OP.mult, op1=OP.mult)
            VE.drain()
            fps_free[4 + tt % 4].inc(VE.engine_nop())
            VE.reduce_max(lnsc[:, 5:6], hsl, axis=AX, apply_absolute_value=True)
            VE.drain()
            VE.tensor_tensor(fst[:, 32 + tt:33 + tt], fst[:, 32 + tt:33 + tt],
                             lnsc[:, 5:6], op=OP.max)
            vc2.inc(VE.drain())
            vc2.wait(SY)
            st_h[hs].inc(SY.dma_start(
                out=d_hsp[tt * 128:(tt + 1) * 128, oc * 512:(oc + 1) * 512],
                in_=hsl), 16)
        mm_guard[0] = t2v.n
    vc2.wait(VE)
    VE.tensor_scalar(fst[:, 32:40], fst[:, 32:40], float(1.0 / 127.0), EPS,
                     op0=OP.mult, op1=OP.add)
    VE.drain()
    VE.reciprocal(fst[:, 40:48], fst[:, 32:40])
    VE.drain()
    VE.tensor_scalar(fst[:, 32:40], fst[:, 32:40], wsml[:, 2:3], None, op0=OP.mult)
    vc2.inc(VE.drain())
    st_h[0].wait(SY)
    st_h[1].wait(SY)
    if stage_done():
        return nc
    bar()
    f2_ps.close()
    f2_ctx.close()
    xq_ctx.close()

    f3_ctx = ExitStack()
    hqfm = f3_ctx.enter_context(nc.sbuf_tensor("hqfm", [128, 64 * NTOK], BF16))
    f3_ps = ExitStack()
    f3tr = [f3_ps.enter_context(nc.psum_tensor("f3tr0", [128, 1024], BF16)),
            f3_ps.enter_context(nc.psum_tensor("f3tr1", [128, 1024], BF16))]
    f3_marks = []
    f3_tr_marks = []
    for tt in range(8):
        for pc in range(4):
            li = tt * 4 + pc
            slot = li % 2
            sl = f32a[:, slot * DM:(slot + 1) * DM]
            if li >= 2:
                SY.wait_ge(vc2.h, f3_marks[li - 2])
            s_fa[slot].inc(SY.dma_start(
                out=sl, in_=d_hsp[tt * 128:(tt + 1) * 128,
                                  pc * DM:(pc + 1) * DM]), 16)
            s_fa[slot].wait(VE)
            if li >= 1:
                VE.wait_ge(t2v.h, f3_tr_marks[li - 1])
            quant_act_tile(sl, fst[:, 40 + tt:41 + tt], bf_a[:, :DM], DM)
            f3_marks.append(vc2.n)
            vc2.wait(TE)
            done = 0
            while done < KCH:
                bi = (done // 8) % 2
                take = min(8, KCH - done)
                tr_free[bi].wait(TE)
                last = None
                for i in range(take):
                    last = TE.transpose(f3tr[bi][:, i * 128:(i + 1) * 128],
                                        bf_a[:, (done + i) * 128:(done + i + 1) * 128],
                                        ident_s[:, :])
                tr_done[bi].inc(last)
                eng = VE if bi == 0 else SE
                tr_done[bi].wait(eng)
                for i in range(take):
                    ic = pc * KCH + done + i
                    col = ic * NTOK + tt * 128
                    if eng is VE:
                        VE.tensor_copy(hqfm[:, col:col + 128],
                                       f3tr[bi][:, i * 128:(i + 1) * 128])
                    else:
                        SE.copy(hqfm[:, col:col + 128],
                                f3tr[bi][:, i * 128:(i + 1) * 128])
                if eng is VE:
                    vc2.inc(VE.drain())
                    tr_free[bi].inc(VE.engine_nop())
                else:
                    sc2.inc(SE.drain())
                    tr_free[bi].inc(SE.drain())
                done += take
            t2v.inc(TE.drain())
            f3_tr_marks.append(t2v.n)
    if stage_done():
        return nc
    bar()
    f3_ps.close()

    f4s_ctx = ExitStack()
    bf16_wd = f4s_ctx.enter_context(nc.sbuf_tensor("bf16_wd", [128, 2 * 512], BF16))
    f4_ps = ExitStack()
    dps = [f4_ps.enter_context(nc.psum_tensor(f"dps{i}", [128, 512], F32))
           for i in range(8)]
    dps_free = [S(nc, f"dpsf{i}") for i in range(8)]
    f4_marks = []
    f4_marks_mm = []
    f4_li = [0]
    for ocd in range(4):
        for ic in range(64):
            li = f4_li[0]
            f4_li[0] += 1
            slot = li % 2
            sl = f32a[:, slot * DM: slot * DM + 512]
            if li >= 2:
                SY.wait_ge(vc2.h, f4_marks[li - 2])
            s_fa[slot].inc(SY.dma_start(
                out=sl, in_=d_wdT[ic * 128:(ic + 1) * 128,
                                  ocd * 512:(ocd + 1) * 512]), 16)
            s_fa[slot].wait(VE)
            bslot = li % 2
            bc2 = slice(bslot * 1024 + 512, bslot * 1024 + 1024)
            VE.tensor_scalar(bf_a[:, bc2], sl, -1.0, wsml[:, 6:7],
                             op0=OP.mult, op1=OP.is_ge)
            VE.drain()
            if li >= 2:
                VE.wait_ge(t2v.h, f4_marks_mm[li - 2])
            vc2.inc(VE.scalar_tensor_tensor(bf16_wd[:, bslot * 512:(bslot + 1) * 512],
                                            sl, wsml[:, 6:7], bf_a[:, bc2],
                                            op0=OP.is_ge, op1=OP.subtract))
            f4_marks.append(vc2.n)
            vc2.wait(TE)
            if ic == 0:
                for tt in range(8):
                    dps_free[tt].wait(TE)
            last = None
            for tt in range(8):
                last = TE.matmul(dps[tt][:, :],
                                 hqfm[:, ic * NTOK + tt * 128: ic * NTOK + (tt + 1) * 128],
                                 bf16_wd[:, bslot * 512:(bslot + 1) * 512],
                                 start=(ic == 0), stop=(ic == 63))
            t2v.inc(last)
            f4_marks_mm.append(t2v.n)
        t2v.wait(VE)
        for tt in range(8):
            slot = tile_ctr[0] % 2
            tile_ctr[0] += 1
            sl = x_sb[:, slot * DM:(slot + 1) * DM]
            xf = s_xf[slot]
            xf.wait(SY)
            r0 = tt * 128
            s_x[slot].inc(SY.dma_start(out=sl[:, :512],
                                       in_=d_rsp[r0:r0 + 128,
                                                 ocd * 512:(ocd + 1) * 512]), 16)
            s_x[slot].wait(VE)
            VE.scalar_tensor_tensor(sl[:, 512:1024], dps[tt][:, :],
                                    fst[:, 32 + tt:33 + tt], sl[:, :512],
                                    op0=OP.mult, op1=OP.add)
            vc2.inc(VE.drain())
            dps_free[tt].inc(VE.engine_nop())
            vc2.wait(SY)
            st_x[slot].inc(SY.dma_start(out=d_out[r0:r0 + 128,
                                                  ocd * 512:(ocd + 1) * 512],
                                        in_=sl[:, 512:1024]), 16)
            st_x[slot].wait(SY)
            xf.inc(SY.nop(), 1)
    bar()
    f4_ps.close()
    f4s_ctx.close()
    f3_ctx.close()
    fscr_ctx.close()
    ffn_ctx.close()

    for sem in (st_x[0], st_x[1], st_h[0], st_h[1], s_x[0], s_x[1],
                s_wab[0], s_wab[1], s_wq[0], s_wq[1], s_fa[0], s_fa[1], s_cst):
        sem.wait(SY)
    ctx.close()
    return nc


def make_in_maps(L0_attn_x, L0_attn_wq, L0_attn_wk, L0_attn_wv, L0_attn_wo,
                 L0_ffn_w_gate, L0_ffn_w_up, L0_ffn_w_down):
    import ml_dtypes
    x = np.ascontiguousarray(np.asarray(L0_attn_x, dtype=np.float32))
    wq = np.ascontiguousarray(np.asarray(L0_attn_wq, dtype=np.float32))
    wk = np.ascontiguousarray(np.asarray(L0_attn_wk, dtype=np.float32))
    wv = np.ascontiguousarray(np.asarray(L0_attn_wv, dtype=np.float32))
    wo = np.ascontiguousarray(np.asarray(L0_attn_wo, dtype=np.float32))
    wgT = np.ascontiguousarray(np.asarray(L0_ffn_w_gate, dtype=np.float32).T)
    wuT = np.ascontiguousarray(np.asarray(L0_ffn_w_up, dtype=np.float32).T)
    wdT = np.ascontiguousarray(np.asarray(L0_ffn_w_down, dtype=np.float32).T)

    ident = np.eye(128, dtype=ml_dtypes.bfloat16)
    masks = np.zeros((4 * 128, CH), dtype=np.float32)
    for t in range(4):
        for i in range(128):
            masks[t * 128 + i, t * 128 + i + 1:] = NEG
    masks = masks.astype(ml_dtypes.bfloat16)

    in_maps = []
    for c in range(8):
        b, j = c // 4, c % 4
        blkA, blkB = j, 7 - j
        xq = np.concatenate([x[b, blkA * QB:(blkA + 1) * QB],
                             x[b, blkB * QB:(blkB + 1) * QB]], axis=0)
        xkv = np.zeros((STEPS * CH, DM), dtype=np.float32)
        xkv[0:CH] = x[b, blkA * QB:(blkA + 1) * QB]
        for s in range(1, blkA + 1):
            xkv[s * CH:(s + 1) * CH] = x[b, (s - 1) * CH:s * CH]
        ob = ASTEPS * CH
        xkv[ob:ob + CH] = x[b, blkB * QB:(blkB + 1) * QB]
        for s in range(1, blkB + 1):
            xkv[ob + s * CH: ob + (s + 1) * CH] = x[b, (s - 1) * CH:s * CH]
        gates = np.zeros((128, STEPS), dtype=np.float32)
        for s in range(ASTEPS):
            if s > blkA:
                gates[:, s] = NEG
        for s in range(BSTEPS):
            if s > blkB:
                gates[:, ASTEPS + s] = NEG
        in_maps.append({
            "xq_rows": np.ascontiguousarray(xq),
            "x_kv": xkv,
            "wq": wq, "wk": wk, "wv": wv, "wo": wo,
            "wgT": wgT, "wuT": wuT, "wdT": wdT,
            "masks": masks, "gates": gates, "ident": ident,
        })

    return in_maps


def assemble_out(results):
    out = np.empty((B, T, DM), dtype=np.float32)
    for c in range(8):
        b, j = c // 4, c % 4
        blkA, blkB = j, 7 - j
        o = results[c]["out"]
        out[b, blkA * QB:(blkA + 1) * QB] = o[:QB]
        out[b, blkB * QB:(blkB + 1) * QB] = o[QB:]
    return out


def kernel(**inputs):
    in_maps = make_in_maps(**inputs)
    nc = build_nc()
    res = run_bass_kernel_spmd(nc, in_maps, core_ids=list(range(8)))
    return assemble_out(res.results)
